# version 11
# baseline (speedup 1.0000x reference)
import contextlib
import numpy as np
import ml_dtypes
import concourse.bass as bass
import concourse.mybir as mybir
from concourse.bass_utils import run_bass_kernel_spmd

F32 = mybir.dt.float32
BF16 = mybir.dt.bfloat16
AF = mybir.ActivationFunctionType
ALU = mybir.AluOpType
AX = mybir.AxisListType

TOWN = 2048
WARM = 512
EXT = WARM + TOWN
NCH = EXT // 512
CH = 512
D = 1024
HID = 2048
INNER = 2048
STATE = 16
FMOE = 4096
NE = 8
FFN = 2816
EPS = 1e-5
MAGIC = float(np.float32(2.0 ** 23))

NT_D = D // 128
NT_H = HID // 128
NT_F = FMOE // 128
NT_C = INNER // 128
NT_FF = FFN // 128

BF = ml_dtypes.bfloat16
DEBUG1 = False
DEBUG2 = False


class Prog:
    def __init__(self, nc, st):
        self.nc = nc
        self.sem = {}
        self.cnt = {}
        for name in ("v", "s", "t", "g", "dg", "dy"):
            self.sem[name] = st.enter_context(nc.semaphore("sem_" + name))
            self.cnt[name] = 0

    def _mark(self, q, inst, n=1):
        self.cnt[q] += n
        inst.then_inc(self.sem[q], n)
        return self.cnt[q]

    def V(self, inst):
        return self._mark("v", inst)

    def S(self, inst):
        return self._mark("s", inst)

    def T(self, inst):
        return self._mark("t", inst)

    def G(self, inst):
        return self._mark("g", inst)

    def dma(self, eng, q, out, in_):
        inst = eng.dma_start(out=out, in_=in_)
        return self._mark(q, inst, 16)

    def wait(self, eng, q, val):
        assert val <= self.cnt[q], (q, val, self.cnt[q])
        if val > 0:
            eng.wait_ge(self.sem[q], val)

    def snap(self):
        return dict(self.cnt)



def _tiles_lhsT(w):
    n, k = w.shape
    t = w.reshape(n // 128, 128, k // 128, 128)
    return np.ascontiguousarray(t.transpose(0, 3, 2, 1).reshape(n // 128, 128, k))


def _colmaj(vec):
    return np.ascontiguousarray(vec.reshape(-1, 128).T)


def host_prepare(w):
    H = {}
    H["wfT"] = _tiles_lhsT(w["L0_attn_W_fwd"])
    H["woT"] = _tiles_lhsT(w["L0_attn_W_out"])
    H["centT"] = np.ascontiguousarray(w["L0_ffn_centroids"].T)
    H["gcol"] = np.ascontiguousarray(
        w["L0_ffn_gamma"].reshape(NE, NT_D, 128).transpose(2, 0, 1).reshape(128, NE * NT_D))
    H["bcol"] = np.ascontiguousarray(
        w["L0_ffn_beta"].reshape(NE, NT_D, 128).transpose(2, 0, 1).reshape(128, NE * NT_D))
    H["b1col"] = np.ascontiguousarray(
        w["L0_ffn_b1"].reshape(NE, NT_F, 128).transpose(2, 0, 1).reshape(128, NE * NT_F))
    H["b2w"] = np.ascontiguousarray(w["L0_ffn_b2"])
    H["w1T"] = np.ascontiguousarray(
        np.stack([_tiles_lhsT(w["L0_ffn_W1"][e]) for e in range(NE)]))
    H["w2T"] = np.ascontiguousarray(
        np.stack([_tiles_lhsT(w["L0_ffn_W2"][e]) for e in range(NE)]))
    H["ipT"] = _tiles_lhsT(w["L1_attn_in_proj_w"])
    H["ipb"] = _colmaj(w["L1_attn_in_proj_b"])
    H["cw"] = np.ascontiguousarray(
        w["L1_attn_conv_w"][:, 0, :].reshape(NT_C, 128, 4).transpose(1, 0, 2).reshape(128, 4 * NT_C))
    H["cbc"] = _colmaj(w["L1_attn_conv_b"])
    H["dtT"] = _tiles_lhsT(w["L1_attn_dt_w"])
    H["dtb"] = _colmaj(w["L1_attn_dt_b"])
    H["alog"] = np.ascontiguousarray(
        w["L1_attn_A_log"].reshape(NT_C, 128, STATE).transpose(1, 2, 0).reshape(128, STATE * NT_C))
    H["bwT"] = np.ascontiguousarray(w["L1_attn_B_w"].T.reshape(NT_C, 128, 16))
    H["cwTT"] = np.ascontiguousarray(w["L1_attn_C_w"].T.reshape(NT_C, 128, 16))
    H["bbr"] = np.ascontiguousarray(w["L1_attn_B_b"].reshape(16, 1))
    H["cbr"] = np.ascontiguousarray(w["L1_attn_C_b"].reshape(16, 1))
    H["dcol"] = _colmaj(w["L1_attn_D"])
    H["owT"] = _tiles_lhsT(w["L1_attn_out_w"])
    H["owb"] = _colmaj(w["L1_attn_out_b"])
    H["wg"] = np.ascontiguousarray(w["L1_ffn_w_gate"])
    H["wu"] = np.ascontiguousarray(w["L1_ffn_w_up"])
    H["wd"] = np.ascontiguousarray(w["L1_ffn_w_down"])
    H["idenf"] = np.eye(128, dtype=np.float32)
    return H



def emit_ln(nc, blk, P, src, w, rows, sqb, ps_s, ps_q, ps_m, ps_r, ones_c,
            ones_r, wait_map=None, ones_sq=None, eps_col=None):
    if ones_sq is None:
        ones_sq = ones_c
    mean = rows[0:1, 0:w]
    var = rows[0:1, CH:CH + w]
    tmp = rows[0:1, 2 * CH:2 * CH + w]

    @blk.tensor
    def _(t):
        if wait_map:
            for q, val in wait_map.items():
                P.wait(t, q, val)
        for i in range(NT_D):
            inst = t.matmul(ps_s[0:1, 0:w], ones_c, src(i),
                            start=(i == 0), stop=(i == NT_D - 1))
        P.T(inst)

    for i in range(NT_D):
        sq_done = [0]
        pv_t = P.cnt["t"]

        @blk.scalar
        def _(s):
            if wait_map and i == 0:
                for q, val in wait_map.items():
                    P.wait(s, q, val)
            P.wait(s, "t", pv_t)
            sq_done[0] = P.S(s.activation(sqb[:, 0:w], src(i), AF.Square))

        @blk.tensor
        def _(t):
            P.wait(t, "s", sq_done[0])
            inst = t.matmul(ps_q[0:1, 0:w], ones_sq, sqb[:, 0:w],
                            start=(i == 0), stop=(i == NT_D - 1))
            if i == NT_D - 1:
                P.T(inst)
    t_s2 = P.cnt["t"]

    v_var = [0]

    @blk.vector
    def _(v):
        P.wait(v, "t", t_s2)
        P.V(v.tensor_scalar(mean, ps_s[0:1, 0:w], 1.0 / D, None, ALU.mult))
        P.V(v.tensor_scalar(var, ps_q[0:1, 0:w], 1.0 / D, None, ALU.mult))
        P.V(v.tensor_tensor(tmp, mean, mean, ALU.mult))
        P.V(v.tensor_tensor(var, var, tmp, ALU.subtract))
        v_var[0] = P.cnt["v"]

    s_r = [0]

    @blk.scalar
    def _(s):
        P.wait(s, "v", v_var[0])
        P.S(s.activation(tmp, var, AF.Ln, bias=eps_col[0:1, 0:1]))
        s_r[0] = P.S(s.activation(tmp, tmp, AF.Exp, scale=-0.5))

    @blk.tensor
    def _(t):
        P.wait(t, "s", s_r[0])
        P.T(t.matmul(ps_m[:, 0:w], ones_r, mean, start=True, stop=True))
        P.T(t.matmul(ps_r[:, 0:w], ones_r, tmp, start=True, stop=True))
    return P.cnt["t"]


def emit_fp4(nc, blk, P, x, out, w, sc, tb, ab, ub_, qb, magic_col=None,
             wait_map=None):
    G = w // 16
    bmax = sc[:, 0:G]
    scale = sc[:, G:2 * G]
    rinv = sc[:, 2 * G:3 * G]
    x3 = x.rearrange("p (g k) -> p g k", k=16)

    v1 = [0]

    @blk.vector
    def _(v):
        if wait_map:
            for q_, val in wait_map.items():
                P.wait(v, q_, val)
        n_rd = P.V(v.tensor_reduce(bmax, x3, AX.X, ALU.max,
                                   apply_absolute_value=True))
        P.wait(v, "v", n_rd)
        P.V(v.tensor_scalar(scale, bmax, 1.0 / 6.0, 1e-12, ALU.mult, ALU.max))
        v1[0] = P.cnt["v"]

    s1 = [0]

    @blk.scalar
    def _(s):
        P.wait(s, "v", v1[0])
        P.S(s.activation(rinv, scale, AF.Ln))
        P.S(s.activation(rinv, rinv, AF.Exp, scale=-1.0))
        P.S(s.activation(ab[:, 0:w], x, AF.Abs))
        s1[0] = P.S(s.activation(ub_[:, 0:w], x, AF.Sign))

    @blk.vector
    def _(v):
        P.wait(v, "s", s1[0])
        rb = rinv.unsqueeze(2).broadcast_to([128, G, 16])
        P.V(v.tensor_tensor(tb[:, 0:w].rearrange("p (g k) -> p g k", k=16),
                            ab[:, 0:w].rearrange("p (g k) -> p g k", k=16),
                            rb, ALU.mult))
        P.V(v.tensor_scalar(tb[:, 0:w], tb[:, 0:w], 6.0, None, ALU.min))
        P.V(v.tensor_scalar(ab[:, 0:w], tb[:, 0:w], 2.0, MAGIC, ALU.mult, ALU.add))
        P.V(v.tensor_scalar(qb[:, 0:w], ab[:, 0:w], MAGIC, 0.5, ALU.subtract,
                            ALU.mult))
        P.V(v.tensor_scalar(qb[:, 0:w], qb[:, 0:w], 2.0, None, ALU.min))
        P.V(v.tensor_scalar(ab[:, 0:w], tb[:, 0:w], MAGIC, None, ALU.add))
        P.V(v.tensor_scalar(ab[:, 0:w], ab[:, 0:w], MAGIC + 2.0, 0.0,
                            ALU.subtract, ALU.max))
        P.V(v.scalar_tensor_tensor(qb[:, 0:w], ab[:, 0:w], 2.0, qb[:, 0:w],
                                   ALU.min, ALU.add))
        P.V(v.tensor_scalar(ab[:, 0:w], tb[:, 0:w], 0.5, MAGIC, ALU.mult, ALU.add))
        P.V(v.tensor_scalar(ab[:, 0:w], ab[:, 0:w], MAGIC + 2.0, 2.0,
                            ALU.subtract, ALU.mult))
        P.V(v.scalar_tensor_tensor(qb[:, 0:w], ab[:, 0:w], 0.0, qb[:, 0:w],
                                   ALU.max, ALU.add))
        P.V(v.tensor_tensor(qb[:, 0:w], qb[:, 0:w], ub_[:, 0:w], ALU.mult))
        sb = scale.unsqueeze(2).broadcast_to([128, G, 16])
        P.V(v.tensor_tensor(out.rearrange("p (g k) -> p g k", k=16),
                            qb[:, 0:w].rearrange("p (g k) -> p g k", k=16),
                            sb, ALU.mult))
    return P.cnt["v"]


def fence(nc, blk, P):
    snap = P.snap()

    for eng_name in ("vector", "scalar", "tensor", "gpsimd", "sync"):
        def _body(e, _snap=snap):
            for q, val in _snap.items():
                P.wait(e, q, val)
        getattr(blk, eng_name)(_body)



def build_launch1():
    nc = bass.Bass("TRN2", target_bir_lowering=False)
    I, O = {}, {}

    def di(name, shape, dt=F32):
        I[name] = nc.dram_tensor(name, list(shape), dt, kind="ExternalInput").ap()

    def do_(name, shape, dt=F32):
        O[name] = nc.dram_tensor(name, list(shape), dt, kind="ExternalOutput").ap()

    di("x_ext", [D, EXT])
    di("m3col", [128, 1])
    di("idenf", [128, 128])
    di("cvals", [128, 2])
    di("wfT", [3 * NT_H, 128, D])
    di("woT", [NT_D, 128, HID])
    di("centT", [D, NE])
    di("gcol", [128, NE * NT_D])
    di("bcol", [128, NE * NT_D])
    di("b1col", [128, NE * NT_F])
    di("b2w", [NE, D])
    di("w1T", [NE, NT_F, 128, D])
    di("w2T", [NE, NT_D, 128, FMOE])
    di("ipT", [32, 128, D])
    di("ipb", [128, 32])
    di("cw", [128, 4 * NT_C])
    di("cbc", [128, NT_C])
    di("dtT", [NT_C, 128, INNER])
    di("dtb", [128, NT_C])
    di("alog", [128, STATE * NT_C])
    di("bwT", [NT_C, 128, 16])
    di("cwTT", [NT_C, 128, 16])
    di("bbr", [16, 1])
    di("cbr", [16, 1])

    do_("x1", [D, EXT])
    if DEBUG1:
        do_("dbg_xn", [D, EXT], BF16)
        do_("dbg_s", [D, EXT], BF16)
        do_("dbg_xhat", [D, EXT], BF16)
        do_("dbg_mrow", [NE, EXT])
        do_("dbg_acc", [D, EXT], BF16)
    do_("xn1f", [D, TOWN])
    do_("xc", [INNER, TOWN], BF16)
    do_("dt", [INNER, TOWN], BF16)
    do_("z", [INNER, TOWN], BF16)
    do_("bm", [16, TOWN])
    do_("cm", [16, TOWN])
    do_("summ", [INNER, 32])

    mrow_d = nc.dram_tensor("mrow_d", [NE, EXT], F32, kind="Internal").ap()
    w1q_d = nc.dram_tensor("w1q_d", [NE, NT_F, 128, NT_D * 128], BF16, kind="Internal").ap()
    w2q_d = nc.dram_tensor("w2q_d", [NE, NT_D, 128, NT_F * 128], BF16, kind="Internal").ap()
    bm_d = nc.dram_tensor("bm_d", [16, TOWN], F32, kind="Internal").ap()

    with contextlib.ExitStack() as st0:
        al = nc.alloc_sbuf_tensor
        ones_c = al("ones_c", [128, 1], F32)
        ones_r = al("ones_r", [1, 128], F32)
        onesb_c = al("onesb_c", [128, 1], BF16)
        m3c = al("m3c", [128, 1], F32)
        iden = al("iden", [128, 128], F32)
        rows = al("rows", [1, 3 * CH], F32)
        cvc = al("cvc", [128, 2], F32)

        blk = st0.enter_context(nc.Block())
        P = Prog(nc, st0)

        @blk.sync
        def _(sy):
            P.dma(sy, "dy", m3c[:], I["m3col"])
            P.dma(sy, "dy", iden[:], I["idenf"])
            P.dma(sy, "dy", cvc[:], I["cvals"])
        d0 = P.cnt["dy"]

        @blk.vector
        def _(v):
            P.wait(v, "dy", d0)
            P.V(v.memset(ones_c[:], 1.0))
            P.V(v.memset(ones_r[:], 1.0))
            P.V(v.memset(onesb_c[:], 1.0))
        v_ones = P.cnt["v"]

        with contextlib.ExitStack() as _sc:
            xn = _sc.enter_context(nc.sbuf_tensor("xn", [128, NT_D * EXT], BF16))
            SW = 2052
            s_sb = _sc.enter_context(nc.sbuf_tensor("s_sb", [128, NT_D * SW], BF16))

            with contextlib.ExitStack() as _sc:
                xa = _sc.enter_context(nc.sbuf_tensor("xa", [128, NT_D * CH], F32))
                sqa = _sc.enter_context(nc.sbuf_tensor("sqa", [128, CH], F32))
                ps_s = _sc.enter_context(nc.psum_tensor("ps_s", [1, CH], F32))
                ps_q = _sc.enter_context(nc.psum_tensor("ps_q", [1, CH], F32))
                ps_m = _sc.enter_context(nc.psum_tensor("ps_m", [128, CH], F32))
                ps_r = _sc.enter_context(nc.psum_tensor("ps_r", [128, CH], F32))
                for ci in range(NCH):
                    c0 = ci * CH
                    pv = P.cnt["v"]

                    @blk.sync
                    def _(sy):
                        P.wait(sy, "v", pv)
                        for i in range(NT_D):
                            P.dma(sy, "dy", xa[:, i * CH:(i + 1) * CH],
                                  I["x_ext"][i * 128:(i + 1) * 128, c0:c0 + CH])
                    dA = P.cnt["dy"]

                    wm = {"dy": dA}
                    if ci == 0:
                        wm["v"] = v_ones
                    t_bc = emit_ln(nc, blk, P,
                                   lambda i: xa[:, i * CH:(i + 1) * CH], CH,
                                   rows, sqa, ps_s, ps_q, ps_m, ps_r,
                                   ones_c[:, 0:1], ones_r[0:1, :],
                                   wait_map=wm, eps_col=cvc[:, 0:1])

                    @blk.vector
                    def _(v):
                        P.wait(v, "t", t_bc)
                        for i in range(NT_D):
                            xt = xa[:, i * CH:(i + 1) * CH]
                            P.V(v.tensor_tensor(xt, xt, ps_m[:, 0:CH], ALU.subtract))
                            P.V(v.tensor_tensor(xn[:, i * EXT + c0:i * EXT + c0 + CH],
                                                xt, ps_r[:, 0:CH], ALU.mult))

            if DEBUG1:
                pv_va = P.cnt["v"]

                @blk.sync
                def _(sy):
                    P.wait(sy, "v", pv_va)
                    for i in range(NT_D):
                        P.dma(sy, "dy", O["dbg_xn"][i * 128:(i + 1) * 128, :],
                              xn[:, i * EXT:(i + 1) * EXT])

            fence(nc, blk, P)
            precast_jobs = []
            for e in range(NE):
                for ft in range(NT_F):
                    precast_jobs.append(("w1", e, ft))
                for dot in range(NT_D):
                    precast_jobs.append(("w2", e, dot))
            pre_cursor = [0]
            pre_done = [0] * NE

            def emit_precast(g, n):
                while n > 0 and pre_cursor[0] < len(precast_jobs):
                    kind, e, idx = precast_jobs[pre_cursor[0]]
                    if kind == "w1":
                        P.dma(g, "dg", w1q_d[e, idx], I["w1T"][e, idx])
                    else:
                        P.dma(g, "dg", w2q_d[e, idx], I["w2T"][e, idx])
                    pre_cursor[0] += 1
                    if pre_cursor[0] % (NT_F + NT_D) == 0:
                        pre_done[(pre_cursor[0] // (NT_F + NT_D)) - 1] = P.cnt["dg"]
                    n -= 1

            with contextlib.ExitStack() as _sc:
                wfb = _sc.enter_context(nc.sbuf_tensor("wfb", [128, 2 * 24 * 128], BF16))
                wob = _sc.enter_context(nc.sbuf_tensor("wob", [128, NT_D * NT_H * 128], BF16))
                eb1 = _sc.enter_context(nc.sbuf_tensor("eb1", [128, CH], F32))
                eb2 = _sc.enter_context(nc.sbuf_tensor("eb2", [128, CH], F32))
                rb = _sc.enter_context(nc.sbuf_tensor("rb", [128, CH], F32))
                gb = _sc.enter_context(nc.sbuf_tensor("gb", [128, CH], F32))
                fb = _sc.enter_context(nc.sbuf_tensor("fb", [128, CH], F32))
                vb = _sc.enter_context(nc.sbuf_tensor("vb", [128, CH], F32))
                hb = _sc.enter_context(nc.sbuf_tensor("hb", [128, CH], F32))
                hchunk = _sc.enter_context(nc.sbuf_tensor("hchunk", [128, NT_H * CH], BF16))
                carry = _sc.enter_context(nc.sbuf_tensor("carry", [128, NT_H], F32))
                ps_a = _sc.enter_context(nc.psum_tensor("ps_a", [128, CH], F32))
                ps_b = _sc.enter_context(nc.psum_tensor("ps_b", [128, CH], F32))
                ps_h = _sc.enter_context(nc.psum_tensor("ps_h", [128, CH], F32))
                ps_o = _sc.enter_context(nc.psum_tensor("ps_o", [128, CH], F32))
                ps_o2 = _sc.enter_context(nc.psum_tensor("ps_o2", [128, CH], F32))

                @blk.vector
                def _(v):
                    P.V(v.memset(carry[:], 0.0))

                @blk.gpsimd
                def _(g):
                    for dot in range(NT_D):
                        P.dma(g, "dg",
                              wob[:, dot * NT_H * 128:(dot + 1) * NT_H * 128],
                              I["woT"][dot])
                dg_wo = P.cnt["dg"]

                for ci in range(NCH):
                    c0 = ci * CH
                    for lt in range(NT_H):
                        par = (ci * NT_H + lt) % 2
                        wsl = wfb[:, par * 24 * 128:(par + 1) * 24 * 128]
                        pv_t = P.cnt["t"]

                        @blk.gpsimd
                        def _(g):
                            P.wait(g, "t", pv_t)
                            for sl in range(3):
                                P.dma(g, "dg",
                                      wsl[:, sl * 8 * 128:(sl + 1) * 8 * 128],
                                      I["wfT"][sl * NT_H + lt])
                            emit_precast(g, 4)
                        dg_w = P.cnt["dg"]
                        s_prev = P.cnt["s"]

                        @blk.tensor
                        def _(t):
                            P.wait(t, "dg", dg_w)
                            P.wait(t, "s", s_prev)
                            for sl, ps in ((0, ps_a), (1, ps_b), (2, ps_h)):
                                for kt in range(NT_D):
                                    inst = t.matmul(
                                        ps[:, 0:CH],
                                        wsl[:, (sl * 8 + kt) * 128:(sl * 8 + kt + 1) * 128],
                                        xn[:, kt * EXT + c0:kt * EXT + c0 + CH],
                                        start=(kt == 0), stop=(kt == NT_D - 1))
                                P.T(inst)
                        t_g = P.cnt["t"]
                        v_pb = P.cnt["v"]

                        s_e = [0]

                        @blk.scalar
                        def _(s):
                            P.wait(s, "t", t_g)
                            P.wait(s, "v", v_pb)
                            P.S(s.activation(eb1[:], ps_a[:, 0:CH], AF.Exp, scale=-1.0))
                            P.S(s.activation(eb2[:], ps_b[:, 0:CH], AF.Exp, scale=-1.0))
                            P.S(s.activation(gb[:], ps_h[:, 0:CH], AF.Exp, scale=-1.0))
                            P.S(s.activation(gb[:], gb[:], AF.Ln, bias=1.0))
                            s_e[0] = P.S(s.activation(gb[:], gb[:], AF.Exp, scale=-1.0))

                        v_d = [0]

                        @blk.vector
                        def _(v):
                            P.wait(v, "s", s_e[0])
                            P.V(v.scalar_tensor_tensor(rb[:], eb1[:], 2.0, eb2[:],
                                                       ALU.add, ALU.add))
                            v_d[0] = P.cnt["v"]

                        s_r = [0]

                        @blk.scalar
                        def _(s):
                            P.wait(s, "v", v_d[0])
                            P.S(s.activation(rb[:], rb[:], AF.Ln))
                            P.S(s.activation(rb[:], rb[:], AF.Exp, scale=-1.0))
                            s_r[0] = P.S(s.activation(fb[:], ps_h[:, 0:CH], AF.Relu))

                        @blk.vector
                        def _(v):
                            P.wait(v, "s", s_r[0])
                            P.V(v.tensor_scalar(gb[:], gb[:], 0.5, None, ALU.min))
                            P.V(v.tensor_tensor(gb[:], gb[:], fb[:], ALU.add))
                            P.V(v.scalar_tensor_tensor(fb[:], eb2[:], 1.0, rb[:],
                                                       ALU.add, ALU.mult))
                            P.V(v.scalar_tensor_tensor(vb[:], eb1[:], 1.0, rb[:],
                                                       ALU.add, ALU.mult))
                            P.V(v.tensor_tensor(vb[:], vb[:], gb[:], ALU.mult))
                            if ci == 0:
                                P.V(v.tensor_scalar(fb[:], fb[:], m3c[:, 0:1],
                                                    None, ALU.mult))
                                P.V(v.tensor_scalar(vb[:], vb[:], m3c[:, 0:1],
                                                    None, ALU.mult))
                            n_sc = P.V(v.tensor_tensor_scan(hb[:], fb[:], vb[:],
                                                     carry[:, lt:lt + 1],
                                                     ALU.mult, ALU.add))
                            P.wait(v, "v", n_sc)
                            P.V(v.tensor_copy(carry[:, lt:lt + 1], hb[:, CH - 1:CH]))
                            P.V(v.tensor_copy(hchunk[:, lt * CH:(lt + 1) * CH], hb[:]))

                    v_h = P.cnt["v"]
                    for dot in range(NT_D):
                        ps = ps_o if dot % 2 == 0 else ps_o2
                        pv_v = P.cnt["v"]
                        t_mm = [0]

                        @blk.tensor
                        def _(t):
                            P.wait(t, "v", v_h)
                            P.wait(t, "v", pv_v)
                            if ci == 0 and dot == 0:
                                P.wait(t, "dg", dg_wo)
                            for kt in range(NT_H):
                                inst = t.matmul(
                                    ps[:, 0:CH],
                                    wob[:, (dot * NT_H + kt) * 128:(dot * NT_H + kt + 1) * 128],
                                    hchunk[:, kt * CH:(kt + 1) * CH],
                                    start=(kt == 0), stop=(kt == NT_H - 1))
                            t_mm[0] = P.T(inst)

                        @blk.vector
                        def _(v):
                            P.wait(v, "t", t_mm[0])
                            if ci == 0:
                                P.V(v.tensor_copy(s_sb[:, dot * SW:dot * SW + 4],
                                                  ps[:, 508:512]))
                            else:
                                P.V(v.tensor_copy(
                                    s_sb[:, dot * SW + c0 - 508:
                                         dot * SW + c0 - 508 + CH],
                                    ps[:, 0:CH]))

            fence(nc, blk, P)
            with contextlib.ExitStack() as _sc:
                xhat = _sc.enter_context(nc.sbuf_tensor("xhat", [128, NT_D * EXT], BF16))
                centb = _sc.enter_context(nc.sbuf_tensor("centb", [128, NT_D * NE], BF16))
                gcols = _sc.enter_context(nc.sbuf_tensor("gcols", [128, NE * NT_D], F32))
                bcols = _sc.enter_context(nc.sbuf_tensor("bcols", [128, NE * NT_D], F32))
                b1cols = _sc.enter_context(nc.sbuf_tensor("b1cols", [128, NE * NT_F], F32))
                b2sb = _sc.enter_context(nc.sbuf_tensor("b2sb", [NE, D], F32))
                sqc = _sc.enter_context(nc.sbuf_tensor("sqc", [128, CH], F32))
                alph = _sc.enter_context(nc.sbuf_tensor("alph", [128, 2], F32))
                msk = _sc.enter_context(nc.sbuf_tensor("msk", [128, NE], F32))
                msk_ev = _sc.enter_context(nc.sbuf_tensor("msk_ev", [NE, 128], F32))
                w1b = _sc.enter_context(nc.sbuf_tensor("w1b", [128, 2 * 8 * 128], BF16))
                w2b = _sc.enter_context(nc.sbuf_tensor("w2b", [128, 2 * 16 * 128], BF16))
                heb = _sc.enter_context(nc.sbuf_tensor("heb", [128, 2 * NT_D * CH], BF16))
                ub = _sc.enter_context(nc.sbuf_tensor("ub", [128, NT_F * CH], BF16))
                accb = _sc.enter_context(nc.sbuf_tensor("accb", [128, NT_D * CH], BF16))
                msb = _sc.enter_context(nc.sbuf_tensor("msb", [128, 2 * CH], BF16))
                mr8 = _sc.enter_context(nc.sbuf_tensor("mr8", [NE, CH], F32))
                tfin = _sc.enter_context(nc.sbuf_tensor("tfin", [128, CH], F32))
                xrow = _sc.enter_context(nc.sbuf_tensor("xrow", [128, CH], F32))
                ps_u = _sc.enter_context(nc.psum_tensor("ps_u", [128, CH], F32))
                ps_u2 = _sc.enter_context(nc.psum_tensor("ps_u2", [128, CH], F32))
                ps_y = _sc.enter_context(nc.psum_tensor("ps_y", [128, CH], F32))
                ps_y2 = _sc.enter_context(nc.psum_tensor("ps_y2", [128, CH], F32))
                ps_k1 = _sc.enter_context(nc.psum_tensor("ps_k1", [128, CH], F32))
                ps_b2 = _sc.enter_context(nc.psum_tensor("ps_b2", [128, CH], F32))

                @blk.gpsimd
                def _(g):
                    for i in range(NT_D):
                        P.dma(g, "dg", centb[:, i * NE:(i + 1) * NE],
                              I["centT"][i * 128:(i + 1) * 128, :])
                    emit_precast(g, 10 ** 9)
                dg_c = P.cnt["dg"]
                assert pre_cursor[0] == len(precast_jobs) and all(pre_done)

                @blk.sync
                def _(sy):
                    P.dma(sy, "dy", gcols[:], I["gcol"])
                    P.dma(sy, "dy", bcols[:], I["bcol"])
                    P.dma(sy, "dy", b1cols[:], I["b1col"])
                    P.dma(sy, "dy", b2sb[:], I["b2w"])
                dy_c = P.cnt["dy"]

                _scp = contextlib.ExitStack()
                ps_sc = _scp.enter_context(nc.psum_tensor("ps_sc", [128, NE], F32))
                ps_tr = _scp.enter_context(nc.psum_tensor("ps_tr", [NE, 128], F32))
                for tsl in range(EXT // 128):
                    o = tsl * 128
                    pv_v = P.cnt["v"]
                    t_sc = [0]

                    @blk.tensor
                    def _(t):
                        if tsl == 0:
                            P.wait(t, "dg", dg_c)
                        P.wait(t, "v", pv_v)
                        for kt in range(NT_D):
                            inst = t.matmul(ps_sc[:, 0:NE],
                                            xn[:, kt * EXT + o:kt * EXT + o + 128],
                                            centb[:, kt * NE:(kt + 1) * NE],
                                            start=(kt == 0), stop=(kt == NT_D - 1))
                        t_sc[0] = P.T(inst)

                    v_mx = [0]

                    @blk.vector
                    def _(v):
                        P.wait(v, "t", t_sc[0])
                        P.V(v.tensor_reduce(alph[:, 0:1], ps_sc[:, 0:NE], AX.X,
                                            ALU.max))
                        v_mx[0] = P.cnt["v"]

                    s_al = [0]

                    @blk.scalar
                    def _(s):
                        P.wait(s, "v", v_mx[0])
                        P.S(s.activation(alph[:, 1:2], alph[:, 0:1], AF.Exp,
                                         scale=-1.0))
                        P.S(s.activation(alph[:, 1:2], alph[:, 1:2], AF.Ln,
                                         bias=1.0))
                        s_al[0] = P.S(s.activation(alph[:, 1:2], alph[:, 1:2],
                                                   AF.Exp, scale=-1.0))

                    v_m = [0]

                    @blk.vector
                    def _(v):
                        P.wait(v, "s", s_al[0])
                        P.V(v.tensor_scalar(msk[:], ps_sc[:, 0:NE], alph[:, 0:1],
                                            alph[:, 1:2], ALU.is_equal, ALU.mult))
                        v_m[0] = P.cnt["v"]

                    t_tr = [0]
                    pv_dy = P.cnt["dy"]

                    @blk.tensor
                    def _(t):
                        P.wait(t, "v", v_m[0])
                        t_tr[0] = P.T(t.transpose(ps_tr[:, :], msk[:], iden[:]))

                    s_tr = [0]

                    @blk.scalar
                    def _(s):
                        P.wait(s, "t", t_tr[0])
                        P.wait(s, "dy", pv_dy)
                        s_tr[0] = P.S(s.activation(msk_ev[:, :], ps_tr[:, :],
                                                   AF.Copy))

                    @blk.sync
                    def _(sy):
                        P.wait(sy, "s", s_tr[0])
                        P.dma(sy, "dy", mrow_d[:, o:o + 128], msk_ev[:, :])
                dy_mr = P.cnt["dy"]
                _scp.close()
                fence(nc, blk, P)

                with contextlib.ExitStack() as _sc:
                    ps_s2 = _sc.enter_context(nc.psum_tensor("ps_s2", [1, CH], F32))
                    ps_q2 = _sc.enter_context(nc.psum_tensor("ps_q2", [1, CH], F32))
                    for ci in range(NCH):
                        c0 = ci * CH
                        pv_v = P.cnt["v"]
                        t_bc = emit_ln(
                            nc, blk, P,
                            lambda i: xn[:, i * EXT + c0:i * EXT + c0 + CH], CH,
                            rows, sqc, ps_s2, ps_q2, ps_u, ps_y,
                            onesb_c[:, 0:1], ones_r[0:1, :],
                            wait_map={"v": pv_v}, ones_sq=ones_c[:, 0:1],
                            eps_col=cvc[:, 0:1])

                        @blk.vector
                        def _(v):
                            P.wait(v, "t", t_bc)
                            for i in range(NT_D):
                                P.V(v.tensor_tensor(
                                    sqc[:], xn[:, i * EXT + c0:i * EXT + c0 + CH],
                                    ps_u[:, 0:CH], ALU.subtract))
                                P.V(v.tensor_tensor(
                                    xhat[:, i * EXT + c0:i * EXT + c0 + CH],
                                    sqc[:], ps_y[:, 0:CH], ALU.mult))

                if DEBUG1:
                    pv_vb = P.cnt["v"]

                    @blk.sync
                    def _(sy):
                        P.wait(sy, "v", pv_vb)
                        for i in range(NT_D):
                            P.dma(sy, "dy",
                                  O["dbg_s"][i * 128:(i + 1) * 128, 508:EXT],
                                  s_sb[:, i * SW:(i + 1) * SW])
                            P.dma(sy, "dy", O["dbg_xhat"][i * 128:(i + 1) * 128, :],
                                  xhat[:, i * EXT:(i + 1) * EXT])
                        P.dma(sy, "dy", O["dbg_mrow"], mrow_d)

                for ci in range(NCH):
                    c0 = ci * CH
                    v_acc0 = [0]

                    @blk.vector
                    def _(v):
                        P.V(v.memset(accb[:], 0.0))
                        v_acc0[0] = P.cnt["v"]

                    v_he_arr = {}

                    def emit_prep(e):
                        par = e % 2
                        hslot = heb[:, par * NT_D * CH:(par + 1) * NT_D * CH]
                        mslot = msb[:, par * CH:(par + 1) * CH]
                        pv_t = P.cnt["t"]
                        dy_m = [0]

                        @blk.sync
                        def _(sy):
                            P.wait(sy, "dy", dy_mr)
                            P.wait(sy, "t", pv_t)
                            P.dma(sy, "dy", rows[0:1, 0:CH],
                                  mrow_d[e:e + 1, c0:c0 + CH])
                            dy_m[0] = P.cnt["dy"]

                        pv_v2 = P.cnt["v"]
                        t_mb = [0]

                        @blk.tensor
                        def _(t):
                            P.wait(t, "dy", dy_m[0])
                            P.wait(t, "v", pv_v2)
                            if ci == 0 and e == 0:
                                P.wait(t, "dy", dy_c)
                            t_mb[0] = P.T(t.matmul(ps_k1[:, 0:CH], ones_r[0:1, :],
                                                   rows[0:1, 0:CH],
                                                   start=True, stop=True))

                        @blk.vector
                        def _(v):
                            P.wait(v, "t", t_mb[0])
                            P.V(v.tensor_copy(mslot, ps_k1[:, 0:CH]))
                            for i in range(NT_D):
                                P.V(v.tensor_scalar(
                                    hslot[:, i * CH:(i + 1) * CH],
                                    xhat[:, i * EXT + c0:i * EXT + c0 + CH],
                                    gcols[:, e * NT_D + i:e * NT_D + i + 1],
                                    bcols[:, e * NT_D + i:e * NT_D + i + 1],
                                    ALU.mult, ALU.add))
                            v_he_arr[e] = P.cnt["v"]

                    emit_prep(0)
                    for e in range(NE):
                        par_e = e % 2
                        hcur = heb[:, par_e * NT_D * CH:(par_e + 1) * NT_D * CH]
                        mcur = msb[:, par_e * CH:(par_e + 1) * CH]
                        v_he = [v_he_arr[e]]

                        for ft in range(NT_F):
                            par = ft % 2
                            wsl = w1b[:, par * 8 * 128:(par + 1) * 8 * 128]
                            psu = ps_u if ft % 2 == 0 else ps_u2
                            pv_t = P.cnt["t"]

                            @blk.sync
                            def _(sy):
                                P.wait(sy, "t", pv_t)
                                if ci == 0 and ft == 0:
                                    P.wait(sy, "dg", pre_done[e])
                                P.dma(sy, "dy", wsl, w1q_d[e, ft])
                            dg_1 = P.cnt["dy"]
                            s_uev = P.cnt["s"]
                            t_u = [0]

                            @blk.tensor
                            def _(t):
                                P.wait(t, "dy", dg_1)
                                P.wait(t, "v", v_he[0])
                                P.wait(t, "s", s_uev)
                                for kt in range(NT_D):
                                    inst = t.matmul(psu[:, 0:CH],
                                                    wsl[:, kt * 128:(kt + 1) * 128],
                                                    hcur[:, kt * CH:(kt + 1) * CH],
                                                    start=(kt == 0),
                                                    stop=(kt == NT_D - 1))
                                t_u[0] = P.T(inst)

                            @blk.scalar
                            def _(s):
                                P.wait(s, "t", t_u[0])
                                P.S(s.activation(
                                    ub[:, ft * CH:(ft + 1) * CH], psu[:, 0:CH],
                                    AF.Relu,
                                    bias=b1cols[:, e * NT_F + ft:e * NT_F + ft + 1]))
                        s_u = P.cnt["s"]
                        if e + 1 < NE:
                            emit_prep(e + 1)

                        for dot in range(NT_D):
                            ps = ps_y if dot % 2 == 0 else ps_y2
                            v_yev = P.cnt["v"]
                            t_y = [0]
                            for half in range(2):
                                par = half
                                w2sl = w2b[:, par * 16 * 128:(par + 1) * 16 * 128]
                                pv_t = P.cnt["t"]

                                @blk.sync
                                def _(sy):
                                    P.wait(sy, "t", pv_t)
                                    P.dma(sy, "dy", w2sl,
                                          w2q_d[e, dot][:,
                                              half * 16 * 128:(half + 1) * 16 * 128])
                                dg_2 = P.cnt["dy"]

                                @blk.tensor
                                def _(t):
                                    P.wait(t, "dy", dg_2)
                                    if half == 0:
                                        P.wait(t, "s", s_u)
                                        P.wait(t, "v", v_yev)
                                    for kt in range(16):
                                        gk = half * 16 + kt
                                        inst = t.matmul(
                                            ps[:, 0:CH],
                                            w2sl[:, kt * 128:(kt + 1) * 128],
                                            ub[:, gk * CH:(gk + 1) * CH],
                                            start=(gk == 0), stop=(gk == NT_F - 1))
                                    if half == 1:
                                        t_y[0] = P.T(inst)

                            @blk.vector
                            def _(v):
                                P.wait(v, "t", t_y[0])
                                P.V(v.tensor_tensor(sqc[:], ps[:, 0:CH], mcur,
                                                    ALU.mult))
                                P.V(v.tensor_tensor(
                                    accb[:, dot * CH:(dot + 1) * CH],
                                    accb[:, dot * CH:(dot + 1) * CH],
                                    sqc[:], ALU.add))

                    if DEBUG1:
                        pv_vc = P.cnt["v"]

                        @blk.sync
                        def _(sy):
                            P.wait(sy, "v", pv_vc)
                            for i in range(NT_D):
                                P.dma(sy, "dy",
                                      O["dbg_acc"][i * 128:(i + 1) * 128,
                                                   c0:c0 + CH],
                                      accb[:, i * CH:(i + 1) * CH])

                    pv_t = P.cnt["t"]
                    dy_m8 = [0]

                    @blk.sync
                    def _(sy):
                        P.wait(sy, "t", pv_t)
                        P.dma(sy, "dy", mr8[:, 0:CH], mrow_d[:, c0:c0 + CH])
                        dy_m8[0] = P.cnt["dy"]

                    fo = 508 if ci == 0 else 0
                    fw = 4 if ci == 0 else CH
                    so = c0 + fo - 508
                    for dot in range(NT_D):
                        pv_v = P.cnt["v"]
                        t_b2m = [0]

                        @blk.tensor
                        def _(t):
                            P.wait(t, "dy", dy_m8[0])
                            P.wait(t, "v", pv_v)
                            t_b2m[0] = P.T(t.matmul(
                                ps_b2[:, 0:fw], b2sb[:, dot * 128:(dot + 1) * 128],
                                mr8[:, fo:fo + fw], start=True, stop=True))

                        dy_x = [0]

                        @blk.sync
                        def _(sy):
                            P.dma(sy, "dy", xrow[:, 0:fw],
                                  I["x_ext"][dot * 128:(dot + 1) * 128,
                                             c0 + fo:c0 + fo + fw])
                            dy_x[0] = P.cnt["dy"]

                        v_t = [0]

                        @blk.vector
                        def _(v):
                            P.wait(v, "t", t_b2m[0])
                            P.V(v.tensor_tensor(tfin[:, 0:fw],
                                                accb[:, dot * CH + fo:dot * CH + fo + fw],
                                                ps_b2[:, 0:fw], ALU.add))
                            P.V(v.tensor_tensor(
                                tfin[:, 0:fw], tfin[:, 0:fw],
                                xn[:, dot * EXT + c0 + fo:dot * EXT + c0 + fo + fw],
                                ALU.add))
                            v_t[0] = P.cnt["v"]

                        s_g = [0]

                        @blk.scalar
                        def _(s):
                            P.wait(s, "v", v_t[0])
                            P.S(s.activation(sqc[:, 0:fw], tfin[:, 0:fw], AF.Exp,
                                             scale=-1.0))
                            P.S(s.activation(sqc[:, 0:fw], sqc[:, 0:fw], AF.Ln,
                                             bias=1.0))
                            s_g[0] = P.S(s.activation(sqc[:, 0:fw], sqc[:, 0:fw],
                                                      AF.Exp, scale=-1.0))

                        v_x1 = [0]

                        @blk.vector
                        def _(v):
                            P.wait(v, "s", s_g[0])
                            P.wait(v, "dy", dy_x[0])
                            P.V(v.tensor_tensor(tfin[:, 0:fw], tfin[:, 0:fw],
                                                sqc[:, 0:fw], ALU.mult))
                            P.V(v.tensor_tensor(
                                tfin[:, 0:fw], s_sb[:, dot * SW + so:dot * SW + so + fw],
                                tfin[:, 0:fw], ALU.mult))
                            P.V(v.tensor_tensor(xrow[:, 0:fw], xrow[:, 0:fw],
                                                tfin[:, 0:fw], ALU.add))
                            v_x1[0] = P.cnt["v"]

                        @blk.sync
                        def _(sy):
                            P.wait(sy, "v", v_x1[0])
                            P.dma(sy, "dy",
                                  O["x1"][dot * 128:(dot + 1) * 128,
                                          c0 + fo:c0 + fo + fw],
                                  xrow[:, 0:fw])

        fence(nc, blk, P)
        with nc.sbuf_tensor("xcsb", [128, NT_C * TOWN], BF16) as xcsb:
            with contextlib.ExitStack() as _sc:
                xn1 = _sc.enter_context(nc.sbuf_tensor("xn1", [128, NT_D * EXT], BF16))
                xa2 = _sc.enter_context(nc.sbuf_tensor("xa2", [128, NT_D * CH], F32))
                sqa2 = _sc.enter_context(nc.sbuf_tensor("sqa2", [128, CH], F32))
                ipbuf = _sc.enter_context(nc.sbuf_tensor("ipbuf", [128, 2 * 8 * 128], BF16))
                ipbsb = _sc.enter_context(nc.sbuf_tensor("ipbsb", [128, 32], F32))
                xp = _sc.enter_context(nc.sbuf_tensor("xp", [128, NT_C * 2052], BF16))
                zev = _sc.enter_context(nc.sbuf_tensor("zev", [128, CH], BF16))
                cwsb = _sc.enter_context(nc.sbuf_tensor("cwsb", [128, 4 * NT_C], F32))
                cbsb = _sc.enter_context(nc.sbuf_tensor("cbsb", [128, NT_C], F32))
                cvb = _sc.enter_context(nc.sbuf_tensor("cvb", [128, CH], F32))
                cv2 = _sc.enter_context(nc.sbuf_tensor("cv2", [128, CH], F32))
                cbneg = _sc.enter_context(nc.sbuf_tensor("cbneg", [128, NT_C], F32))
                ps_s3 = _sc.enter_context(nc.psum_tensor("ps_s3", [1, CH], F32))
                ps_q3 = _sc.enter_context(nc.psum_tensor("ps_q3", [1, CH], F32))
                ps_m3 = _sc.enter_context(nc.psum_tensor("ps_m3", [128, CH], F32))
                ps_r3 = _sc.enter_context(nc.psum_tensor("ps_r3", [128, CH], F32))
                ps_ip = _sc.enter_context(nc.psum_tensor("ps_ip", [128, CH], F32))
                ps_ip2 = _sc.enter_context(nc.psum_tensor("ps_ip2", [128, CH], F32))

                @blk.sync
                def _(sy):
                    P.dma(sy, "dy", ipbsb[:], I["ipb"])
                    P.dma(sy, "dy", cwsb[:], I["cw"])
                    P.dma(sy, "dy", cbsb[:], I["cbc"])
                dy_ip = P.cnt["dy"]

                @blk.vector
                def _(v):
                    P.wait(v, "dy", dy_ip)
                    P.V(v.tensor_scalar(cbneg[:], cbsb[:], -1.0, None, ALU.mult))
                    for i in range(NT_D):
                        P.V(v.memset(xn1[:, i * EXT:i * EXT + 508], 0.0))

                for ci in range(NCH):
                    lo = 508 if ci == 0 else ci * CH
                    lw = 4 if ci == 0 else CH
                    pv = P.cnt["v"]

                    @blk.sync
                    def _(sy):
                        P.wait(sy, "v", pv)
                        for i in range(NT_D):
                            P.dma(sy, "dy", xa2[:, i * CH:i * CH + lw],
                                  O["x1"][i * 128:(i + 1) * 128, lo:lo + lw])
                    dA = P.cnt["dy"]

                    t_bc = emit_ln(nc, blk, P,
                                   lambda i: xa2[:, i * CH:i * CH + lw], lw,
                                   rows, sqa2, ps_s3, ps_q3, ps_m3, ps_r3,
                                   ones_c[:, 0:1], ones_r[0:1, :],
                                   wait_map={"dy": dA}, eps_col=cvc[:, 0:1])

                    v_ln = [0]

                    @blk.vector
                    def _(v):
                        P.wait(v, "t", t_bc)
                        for i in range(NT_D):
                            xt = xa2[:, i * CH:i * CH + lw]
                            P.V(v.tensor_tensor(xt, xt, ps_m3[:, 0:lw], ALU.subtract))
                            P.V(v.tensor_tensor(xt, xt, ps_r3[:, 0:lw], ALU.mult))
                            P.V(v.tensor_copy(
                                xn1[:, i * EXT + lo:i * EXT + lo + lw], xt))
                        v_ln[0] = P.cnt["v"]

                    if ci >= 1:
                        @blk.sync
                        def _(sy):
                            P.wait(sy, "v", v_ln[0])
                            for i in range(NT_D):
                                P.dma(sy, "dy",
                                      O["xn1f"][i * 128:(i + 1) * 128,
                                                lo - CH:lo],
                                      xa2[:, i * CH:(i + 1) * CH])

                ip_chunks = [(508, 4)] + [(512 + k * CH, CH) for k in range(4)]
                for (c0, w) in ip_chunks:
                    for rt in range(32):
                        par = rt % 2
                        wsl = ipbuf[:, par * 8 * 128:(par + 1) * 8 * 128]
                        pv_t = P.cnt["t"]

                        @blk.gpsimd
                        def _(g):
                            P.wait(g, "t", pv_t)
                            P.dma(g, "dg", wsl,
                                  I["ipT"][rt])
                        dg_i = P.cnt["dg"]
                        ps = ps_ip if rt % 2 == 0 else ps_ip2
                        v_pv = P.cnt["v"]
                        t_ip = [0]

                        @blk.tensor
                        def _(t):
                            P.wait(t, "dg", dg_i)
                            P.wait(t, "v", v_pv)
                            if c0 == 508 and rt == 0:
                                P.wait(t, "dy", dy_ip)
                            for kt in range(NT_D):
                                inst = t.matmul(
                                    ps[:, 0:w], wsl[:, kt * 128:(kt + 1) * 128],
                                    xn1[:, kt * EXT + c0:kt * EXT + c0 + w],
                                    start=(kt == 0), stop=(kt == NT_D - 1))
                            t_ip[0] = P.T(inst)

                        if rt < NT_C:
                            @blk.vector
                            def _(v):
                                P.wait(v, "t", t_ip[0])
                                P.V(v.tensor_scalar(
                                    xp[:, rt * 2052 + (c0 - 508):
                                       rt * 2052 + (c0 - 508) + w],
                                    ps[:, 0:w], ipbsb[:, rt:rt + 1], None,
                                    ALU.add))
                        elif c0 != 508:
                            zt = rt - NT_C
                            s_z = [0]

                            pv_dyz = P.cnt["dy"]

                            @blk.vector
                            def _(v):
                                P.wait(v, "t", t_ip[0])
                                P.wait(v, "dy", pv_dyz)
                                s_z[0] = P.V(v.tensor_scalar(
                                    zev[:, 0:w], ps[:, 0:w],
                                    ipbsb[:, rt:rt + 1], None, ALU.add))

                            @blk.sync
                            def _(sy):
                                P.wait(sy, "v", s_z[0])
                                P.dma(sy, "dy",
                                      O["z"][zt * 128:(zt + 1) * 128,
                                             c0 - 512:c0 - 512 + w], zev[:, 0:w])

                @blk.vector
                def _(v):
                    for lt in range(NT_C):
                        P.V(v.tensor_scalar(xp[:, lt * 2052 + 1:lt * 2052 + 4],
                                            xp[:, lt * 2052 + 1:lt * 2052 + 4],
                                            m3c[:, 0:1], None, ALU.mult))

                for ci in range(4):
                    for lt in range(NT_C):
                        xpo = lt * 2052 + 4 + ci * CH
                        s_prevcv = P.cnt["s"]
                        v_cv = [0]

                        @blk.vector
                        def _(v):
                            P.wait(v, "s", s_prevcv)
                            P.V(v.tensor_scalar(cvb[:], xp[:, xpo - 3:xpo - 3 + CH],
                                                cwsb[:, lt * 4:lt * 4 + 1], None,
                                                ALU.mult))
                            for j in range(1, 4):
                                P.V(v.scalar_tensor_tensor(
                                    cvb[:], xp[:, xpo - 3 + j:xpo - 3 + j + CH],
                                    cwsb[:, lt * 4 + j:lt * 4 + j + 1], cvb[:],
                                    ALU.mult, ALU.add))
                            v_cv[0] = P.cnt["v"]

                        s_cv = [0]

                        @blk.scalar
                        def _(s):
                            P.wait(s, "v", v_cv[0])
                            P.S(s.activation(cv2[:], cvb[:], AF.Exp, scale=-1.0,
                                             bias=cbneg[:, lt:lt + 1]))
                            P.S(s.activation(cv2[:], cv2[:], AF.Ln, bias=1.0))
                            s_cv[0] = P.S(s.activation(cv2[:], cv2[:], AF.Exp,
                                                       scale=-1.0))

                        v_cv2 = [0]

                        @blk.vector
                        def _(v):
                            P.wait(v, "s", s_cv[0])
                            P.V(v.tensor_scalar(cvb[:], cvb[:], cbsb[:, lt:lt + 1],
                                                None, ALU.add))
                            v_cv2[0] = P.V(v.tensor_tensor(
                                xcsb[:, lt * TOWN + ci * CH:lt * TOWN + (ci + 1) * CH],
                                cvb[:], cv2[:], ALU.mult))

                        @blk.sync
                        def _(sy):
                            P.wait(sy, "v", v_cv2[0])
                            P.dma(sy, "dy",
                                  O["xc"][lt * 128:(lt + 1) * 128,
                                          ci * CH:(ci + 1) * CH],
                                  xcsb[:, lt * TOWN + ci * CH:lt * TOWN + (ci + 1) * CH])

            fence(nc, blk, P)
            with contextlib.ExitStack() as _sc:
                dtw = _sc.enter_context(nc.sbuf_tensor("dtw", [128, 2 * 16 * 128], BF16))
                dtbsb = _sc.enter_context(nc.sbuf_tensor("dtbsb", [128, NT_C], F32))
                t1 = _sc.enter_context(nc.sbuf_tensor("t1", [128, CH], F32))
                t2 = _sc.enter_context(nc.sbuf_tensor("t2", [128, CH], F32))
                dtsb = _sc.enter_context(nc.sbuf_tensor("dtsb", [128, NT_C * TOWN], BF16))
                dtot = _sc.enter_context(nc.sbuf_tensor("dtot", [128, NT_C], F32))
                asb = _sc.enter_context(nc.sbuf_tensor("asb", [128, STATE * NT_C], F32))
                bwsb = _sc.enter_context(nc.sbuf_tensor("bwsb", [128, NT_C * 16], BF16))
                cwsb2 = _sc.enter_context(nc.sbuf_tensor("cwsb2", [128, NT_C * 16], BF16))
                bbsb = _sc.enter_context(nc.sbuf_tensor("bbsb", [16, 1], F32))
                cbsb2 = _sc.enter_context(nc.sbuf_tensor("cbsb2", [16, 1], F32))
                bcr = _sc.enter_context(nc.sbuf_tensor("bcr", [16, CH], F32))
                brow1 = _sc.enter_context(nc.sbuf_tensor("brow1", [1, CH], F32))
                wbuf = _sc.enter_context(nc.sbuf_tensor("wbuf", [128, NT_C * CH], BF16))
                dab16 = _sc.enter_context(nc.sbuf_tensor("dab16", [128, 2 * CH], F32))
                d1b = _sc.enter_context(nc.sbuf_tensor("d1b", [128, CH], F32))
                hsc = _sc.enter_context(nc.sbuf_tensor("hsc", [128, CH], F32))
                st1 = _sc.enter_context(nc.sbuf_tensor("st1", [128, NT_C * STATE], F32))
                summ_sb = _sc.enter_context(nc.sbuf_tensor("summ_sb", [128, NT_C * 32], F32))
                ps_dt = _sc.enter_context(nc.psum_tensor("ps_dt", [128, CH], F32))
                ps_dt2 = _sc.enter_context(nc.psum_tensor("ps_dt2", [128, CH], F32))
                ps_bc = _sc.enter_context(nc.psum_tensor("ps_bc", [16, CH], F32))
                ps_bb = _sc.enter_context(nc.psum_tensor("ps_bb", [128, CH], F32))

                @blk.sync
                def _(sy):
                    P.dma(sy, "dy", dtbsb[:], I["dtb"])
                    P.dma(sy, "dy", bbsb[:], I["bbr"])
                    P.dma(sy, "dy", cbsb2[:], I["cbr"])
                    P.dma(sy, "dy", asb[:], I["alog"])
                dy_d = P.cnt["dy"]

                @blk.gpsimd
                def _(g):
                    for kt in range(NT_C):
                        P.dma(g, "dg", bwsb[:, kt * 16:(kt + 1) * 16], I["bwT"][kt])
                        P.dma(g, "dg", cwsb2[:, kt * 16:(kt + 1) * 16], I["cwTT"][kt])
                dg_bc = P.cnt["dg"]

                s_a = [0]

                @blk.scalar
                def _(s):
                    P.wait(s, "dy", dy_d)
                    s_a[0] = P.S(s.activation(asb[:], asb[:], AF.Exp))

                @blk.vector
                def _(v):
                    P.wait(v, "s", s_a[0])
                    P.V(v.tensor_scalar(asb[:], asb[:], -1.0, None, ALU.mult))
                    P.V(v.memset(dtot[:], 0.0))
                    P.V(v.memset(st1[:], 0.0))

                for ci in range(4):
                    c0 = ci * CH
                    for rt in range(NT_C):
                        par = rt % 2
                        wsl = dtw[:, par * 16 * 128:(par + 1) * 16 * 128]
                        pv_t = P.cnt["t"]

                        @blk.gpsimd
                        def _(g):
                            P.wait(g, "t", pv_t)
                            for piece in range(2):
                                P.dma(g, "dg",
                                      wsl[:, piece * 8 * 128:(piece + 1) * 8 * 128],
                                      I["dtT"][rt][:, piece * 8 * 128:(piece + 1) * 8 * 128])
                        dg_dt = P.cnt["dg"]
                        ps = ps_dt if rt % 2 == 0 else ps_dt2
                        v_pv = P.cnt["v"]
                        s_xc = P.cnt["s"]
                        t_dt = [0]

                        @blk.tensor
                        def _(t):
                            P.wait(t, "dg", dg_dt)
                            P.wait(t, "v", v_pv)
                            if ci == 0 and rt == 0:
                                P.wait(t, "s", s_xc)
                            for kt in range(NT_C):
                                inst = t.matmul(
                                    ps[:, 0:CH], wsl[:, kt * 128:(kt + 1) * 128],
                                    xcsb[:, kt * TOWN + c0:kt * TOWN + c0 + CH],
                                    start=(kt == 0), stop=(kt == NT_C - 1))
                            t_dt[0] = P.T(inst)

                        v_prevdve = P.cnt["v"]
                        s_sp = [0]

                        @blk.scalar
                        def _(s):
                            P.wait(s, "t", t_dt[0])
                            P.wait(s, "v", v_prevdve)
                            P.S(s.activation(t1[:], ps[:, 0:CH], AF.Abs,
                                             bias=dtbsb[:, rt:rt + 1]))
                            P.S(s.activation(t2[:], t1[:], AF.Exp, scale=-1.0))
                            s_sp[0] = P.S(s.activation(t2[:], t2[:], AF.Ln, bias=1.0))

                        @blk.vector
                        def _(v):
                            P.wait(v, "s", s_sp[0])
                            P.V(v.tensor_scalar(t1[:], ps[:, 0:CH],
                                                dtbsb[:, rt:rt + 1], 0.0,
                                                ALU.add, ALU.max))
                            dtt = dtsb[:, rt * TOWN + c0:rt * TOWN + c0 + CH]
                            P.V(v.tensor_tensor(dtt, t1[:], t2[:], ALU.add))
                            n_rd = P.V(v.tensor_reduce(t2[:, 0:1], dtt, AX.X,
                                                       ALU.add))
                            P.wait(v, "v", n_rd)
                            P.V(v.tensor_tensor(dtot[:, rt:rt + 1],
                                                dtot[:, rt:rt + 1], t2[:, 0:1],
                                                ALU.add))
                    v_dt = P.cnt["v"]

                    @blk.sync
                    def _(sy):
                        P.wait(sy, "v", v_dt)
                        for rt in range(NT_C):
                            P.dma(sy, "dy",
                                  O["dt"][rt * 128:(rt + 1) * 128, c0:c0 + CH],
                                  dtsb[:, rt * TOWN + c0:rt * TOWN + c0 + CH])

                    for which, wt, bias_c, out_d in (
                            (0, bwsb, bbsb, O["bm"]), (1, cwsb2, cbsb2, O["cm"])):
                        v_pv = P.cnt["v"]
                        t_b = [0]

                        @blk.tensor
                        def _(t):
                            if ci == 0 and which == 0:
                                P.wait(t, "dg", dg_bc)
                            P.wait(t, "v", v_pv)
                            for kt in range(NT_C):
                                inst = t.matmul(
                                    ps_bc[:, 0:CH], wt[:, kt * 16:(kt + 1) * 16],
                                    xcsb[:, kt * TOWN + c0:kt * TOWN + c0 + CH],
                                    start=(kt == 0), stop=(kt == NT_C - 1))
                            t_b[0] = P.T(inst)

                        v_bc = [0]
                        pv_dy2 = P.cnt["dy"]

                        @blk.vector
                        def _(v):
                            P.wait(v, "t", t_b[0])
                            P.wait(v, "dy", pv_dy2)
                            P.V(v.tensor_scalar(bcr[:, 0:CH], ps_bc[:, 0:CH],
                                                bias_c[:, 0:1], None, ALU.add))
                            v_bc[0] = P.cnt["v"]

                        @blk.sync
                        def _(sy):
                            P.wait(sy, "v", v_bc[0])
                            P.dma(sy, "dy", out_d[:, c0:c0 + CH], bcr[:, 0:CH])
                            if which == 0:
                                P.dma(sy, "dy", bm_d[:, c0:c0 + CH], bcr[:, 0:CH])

                for ci in range(4):
                    c0 = ci * CH
                    v_w = [0]
                    s_xc2 = P.cnt["s"]

                    @blk.vector
                    def _(v):
                        P.wait(v, "s", s_xc2)
                        for lt in range(NT_C):
                            P.V(v.tensor_tensor(
                                wbuf[:, lt * CH:(lt + 1) * CH],
                                dtsb[:, lt * TOWN + c0:lt * TOWN + c0 + CH],
                                xcsb[:, lt * TOWN + c0:lt * TOWN + c0 + CH],
                                ALU.mult))
                        v_w[0] = P.cnt["v"]

                    for si in range(STATE):
                        pv_t = P.cnt["t"]
                        dy_b = [0]

                        @blk.sync
                        def _(sy):
                            P.wait(sy, "t", pv_t)
                            P.dma(sy, "dy", brow1[0:1, 0:CH],
                                  bm_d[si:si + 1, c0:c0 + CH])
                            dy_b[0] = P.cnt["dy"]

                        pv_v = P.cnt["v"]
                        t_bb = [0]

                        @blk.tensor
                        def _(t):
                            P.wait(t, "dy", dy_b[0])
                            P.wait(t, "v", pv_v)
                            t_bb[0] = P.T(t.matmul(ps_bb[:, 0:CH], ones_r[0:1, :],
                                                   brow1[0:1, 0:CH],
                                                   start=True, stop=True))

                        for lt in range(NT_C):
                            dsl = dab16[:, (lt % 2) * CH:(lt % 2 + 1) * CH]
                            pv_v2 = P.cnt["v"]
                            s_da = [0]

                            @blk.scalar
                            def _(s):
                                P.wait(s, "v", pv_v2)
                                s_da[0] = P.S(s.activation(
                                    dsl,
                                    dtsb[:, lt * TOWN + c0:lt * TOWN + c0 + CH],
                                    AF.Exp,
                                    scale=asb[:, si * NT_C + lt:si * NT_C + lt + 1]))

                            @blk.vector
                            def _(v):
                                P.wait(v, "s", s_da[0])
                                P.wait(v, "t", t_bb[0])
                                P.wait(v, "v", v_w[0])
                                P.V(v.tensor_tensor(d1b[:],
                                                    wbuf[:, lt * CH:(lt + 1) * CH],
                                                    ps_bb[:, 0:CH], ALU.mult))
                                n_sc = P.V(v.tensor_tensor_scan(
                                    hsc[:], dsl, d1b[:],
                                    st1[:, lt * STATE + si:lt * STATE + si + 1],
                                    ALU.mult, ALU.add))
                                P.wait(v, "v", n_sc)
                                P.V(v.tensor_copy(
                                    st1[:, lt * STATE + si:lt * STATE + si + 1],
                                    hsc[:, CH - 1:CH]))

                s_pe = [0]
                pv_v = P.cnt["v"]

                @blk.scalar
                def _(s):
                    P.wait(s, "v", pv_v)
                    for si in range(STATE):
                        for lt in range(NT_C):
                            P.S(s.activation(
                                summ_sb[:, lt * 32 + si:lt * 32 + si + 1],
                                dtot[:, lt:lt + 1], AF.Exp,
                                scale=asb[:, si * NT_C + lt:si * NT_C + lt + 1]))
                    s_pe[0] = P.cnt["s"]

                v_fin = [0]

                @blk.vector
                def _(v):
                    P.wait(v, "s", s_pe[0])
                    for lt in range(NT_C):
                        P.V(v.tensor_copy(
                            summ_sb[:, lt * 32 + 16:lt * 32 + 32],
                            st1[:, lt * STATE:(lt + 1) * STATE]))
                    v_fin[0] = P.cnt["v"]

                @blk.sync
                def _(sy):
                    P.wait(sy, "v", v_fin[0])
                    for lt in range(NT_C):
                        P.dma(sy, "dy", O["summ"][lt * 128:(lt + 1) * 128, :],
                              summ_sb[:, lt * 32:(lt + 1) * 32])

        @blk.sync
        def _(sy):
            P.wait(sy, "dy", P.cnt["dy"])

        @blk.gpsimd
        def _(g):
            P.wait(g, "dg", P.cnt["dg"])
            P.wait(g, "dy", P.cnt["dy"])

    return nc



def build_launch2():
    nc = bass.Bass("TRN2", target_bir_lowering=False)
    I, O = {}, {}

    def di(name, shape, dt=F32):
        I[name] = nc.dram_tensor(name, list(shape), dt, kind="ExternalInput").ap()

    def do_(name, shape, dt=F32):
        O[name] = nc.dram_tensor(name, list(shape), dt, kind="ExternalOutput").ap()

    di("x1o", [D, TOWN])
    di("xn1f", [D, TOWN])
    di("xci", [INNER, TOWN], BF16)
    di("dti", [INNER, TOWN], BF16)
    di("zi", [INNER, TOWN], BF16)
    di("bmi", [16, TOWN])
    di("cmi", [16, TOWN])
    di("summs", [4, INNER, 32])
    di("selc", [128, 4])
    di("alog", [128, STATE * NT_C])
    di("dcol", [128, NT_C])
    di("owT", [NT_D, 128, INNER])
    di("owb", [128, NT_D])
    di("wg", [FFN, D])
    di("wu", [FFN, D])
    di("wd", [D, FFN])
    di("idenf", [128, 128])
    di("cvals", [128, 2])

    do_("x2", [D, TOWN])
    if DEBUG2:
        do_("dbg_s1", [D, TOWN], BF16)
        do_("dbg_y", [INNER, TOWN], BF16)
        do_("dbg_xqf", [D, TOWN], BF16)
        do_("dbg_hqf", [FFN, TOWN], BF16)
        do_("dbg_wgq", [NT_FF, 128, NT_D * 128], BF16)

    wgq_d = nc.dram_tensor("wgq_d", [NT_FF, 128, NT_D * 128], BF16, kind="Internal").ap()
    wuq_d = nc.dram_tensor("wuq_d", [NT_FF, 128, NT_D * 128], BF16, kind="Internal").ap()
    wdq_d = nc.dram_tensor("wdq_d", [NT_D, 128, NT_FF * 128], BF16, kind="Internal").ap()

    with contextlib.ExitStack() as st0:
        al = nc.alloc_sbuf_tensor
        ones_r = al("ones_r", [1, 128], F32)
        iden = al("iden", [128, 128], F32)
        idenb = al("idenb", [128, 128], BF16)
        rows = al("rows", [1, 3 * CH], F32)
        s1_sb = al("s1_sb", [128, NT_D * TOWN], BF16)
        ones_c = al("ones_c", [128, 1], F32)
        cvc = al("cvc", [128, 2], F32)

        blk = st0.enter_context(nc.Block())
        P = Prog(nc, st0)

        @blk.sync
        def _(sy):
            P.dma(sy, "dy", iden[:], I["idenf"])
            P.dma(sy, "dy", cvc[:], I["cvals"])
        d0 = P.cnt["dy"]

        @blk.gpsimd
        def _(g):
            P.dma(g, "dg", idenb[:], I["idenf"])
        d0g = P.cnt["dg"]

        @blk.vector
        def _(v):
            P.wait(v, "dy", d0)
            P.V(v.memset(ones_r[:], 1.0))
            P.V(v.memset(ones_c[:], 1.0))

        with contextlib.ExitStack() as _sc:
            summ_sb = _sc.enter_context(nc.sbuf_tensor("summ_sb", [128, NT_C * 32], F32))
            hin = _sc.enter_context(nc.sbuf_tensor("hin", [128, NT_C * STATE], F32))
            hnew = _sc.enter_context(nc.sbuf_tensor("hnew", [128, STATE], F32))
            selsb = _sc.enter_context(nc.sbuf_tensor("selsb", [128, 4], F32))
            asb = _sc.enter_context(nc.sbuf_tensor("asb", [128, STATE * NT_C], F32))
            dcsb = _sc.enter_context(nc.sbuf_tensor("dcsb", [128, NT_C], F32))
            owbsb = _sc.enter_context(nc.sbuf_tensor("owbsb", [128, NT_D], F32))
            owsb = _sc.enter_context(nc.sbuf_tensor("owsb", [128, NT_D * NT_C * 128], BF16))
            xcb = _sc.enter_context(nc.sbuf_tensor("xcb", [128, NT_C * CH], BF16))
            dtb_ = _sc.enter_context(nc.sbuf_tensor("dtb_", [128, NT_C * CH], BF16))
            wbuf = _sc.enter_context(nc.sbuf_tensor("wbuf", [128, NT_C * CH], BF16))
            ybuf = _sc.enter_context(nc.sbuf_tensor("ybuf", [128, NT_C * CH], F32))
            zbuf = _sc.enter_context(nc.sbuf_tensor("zbuf", [128, CH], BF16))
            szb = _sc.enter_context(nc.sbuf_tensor("szb", [128, CH], F32))
            brow1 = _sc.enter_context(nc.sbuf_tensor("brow1", [1, CH], F32))
            crow1 = _sc.enter_context(nc.sbuf_tensor("crow1", [1, CH], F32))
            dab16 = _sc.enter_context(nc.sbuf_tensor("dab16", [128, 2 * CH], F32))
            d1b = _sc.enter_context(nc.sbuf_tensor("d1b", [128, CH], F32))
            hsc = _sc.enter_context(nc.sbuf_tensor("hsc", [128, CH], F32))
            ytmp = _sc.enter_context(nc.sbuf_tensor("ytmp", [128, CH], F32))
            ybf = _sc.enter_context(nc.sbuf_tensor("ybf", [128, NT_C * CH], BF16))
            ps_bb = _sc.enter_context(nc.psum_tensor("ps_bb", [128, CH], F32))
            ps_cb = _sc.enter_context(nc.psum_tensor("ps_cb", [128, CH], F32))
            ps_s1 = _sc.enter_context(nc.psum_tensor("ps_s1", [128, CH], F32))
            ps_s1b = _sc.enter_context(nc.psum_tensor("ps_s1b", [128, CH], F32))

            @blk.sync
            def _(sy):
                P.dma(sy, "dy", selsb[:], I["selc"])
                P.dma(sy, "dy", asb[:], I["alog"])
                P.dma(sy, "dy", dcsb[:], I["dcol"])
                P.dma(sy, "dy", owbsb[:], I["owb"])
            dy_a = P.cnt["dy"]

            @blk.gpsimd
            def _(g):
                for dot in range(NT_D):
                    P.dma(g, "dg",
                          owsb[:, dot * NT_C * 128:(dot + 1) * NT_C * 128],
                          I["owT"][dot])
            dg_ow = P.cnt["dg"]

            s_a = [0]

            @blk.scalar
            def _(s):
                P.wait(s, "dy", dy_a)
                s_a[0] = P.S(s.activation(asb[:], asb[:], AF.Exp))

            @blk.vector
            def _(v):
                P.wait(v, "s", s_a[0])
                P.V(v.tensor_scalar(asb[:], asb[:], -1.0, None, ALU.mult))
                P.V(v.memset(hin[:], 0.0))

            for r in range(3):
                pv_v = P.cnt["v"]
                dy_s = [0]

                @blk.sync
                def _(sy):
                    P.wait(sy, "v", pv_v)
                    for lt in range(NT_C):
                        P.dma(sy, "dy", summ_sb[:, lt * 32:(lt + 1) * 32],
                              I["summs"][r, lt * 128:(lt + 1) * 128, :])
                    dy_s[0] = P.cnt["dy"]

                @blk.vector
                def _(v):
                    P.wait(v, "dy", dy_s[0])
                    for lt in range(NT_C):
                        hv = hin[:, lt * STATE:(lt + 1) * STATE]
                        P.V(v.tensor_tensor(hnew[:, 0:STATE],
                                            summ_sb[:, lt * 32:lt * 32 + 16],
                                            hv, ALU.mult))
                        P.V(v.tensor_tensor(hnew[:, 0:STATE], hnew[:, 0:STATE],
                                            summ_sb[:, lt * 32 + 16:lt * 32 + 32],
                                            ALU.add))
                        P.V(v.tensor_tensor(hnew[:, 0:STATE], hnew[:, 0:STATE],
                                            hv, ALU.subtract))
                        P.V(v.tensor_scalar(hnew[:, 0:STATE], hnew[:, 0:STATE],
                                            selsb[:, r:r + 1], None, ALU.mult))
                        P.V(v.tensor_tensor(hv, hv, hnew[:, 0:STATE], ALU.add))

            for ci in range(4):
                c0 = ci * CH
                pv_v = P.cnt["v"]
                dy_x = [0]

                @blk.sync
                def _(sy):
                    P.wait(sy, "v", pv_v)
                    for lt in range(NT_C):
                        P.dma(sy, "dy", xcb[:, lt * CH:(lt + 1) * CH],
                              I["xci"][lt * 128:(lt + 1) * 128, c0:c0 + CH])
                        P.dma(sy, "dy", dtb_[:, lt * CH:(lt + 1) * CH],
                              I["dti"][lt * 128:(lt + 1) * 128, c0:c0 + CH])
                    dy_x[0] = P.cnt["dy"]

                v_w = [0]

                @blk.vector
                def _(v):
                    P.wait(v, "dy", dy_x[0])
                    for lt in range(NT_C):
                        P.V(v.tensor_tensor(wbuf[:, lt * CH:(lt + 1) * CH],
                                            dtb_[:, lt * CH:(lt + 1) * CH],
                                            xcb[:, lt * CH:(lt + 1) * CH], ALU.mult))
                    P.V(v.memset(ybuf[:], 0.0))
                    v_w[0] = P.cnt["v"]

                for si in range(STATE):
                    pv_t = P.cnt["t"]
                    dy_b = [0]

                    @blk.sync
                    def _(sy):
                        P.wait(sy, "t", pv_t)
                        P.dma(sy, "dy", brow1[0:1, 0:CH],
                              I["bmi"][si:si + 1, c0:c0 + CH])
                        P.dma(sy, "dy", crow1[0:1, 0:CH],
                              I["cmi"][si:si + 1, c0:c0 + CH])
                        dy_b[0] = P.cnt["dy"]

                    pv_v2 = P.cnt["v"]
                    t_bb = [0]

                    @blk.tensor
                    def _(t):
                        P.wait(t, "dy", dy_b[0])
                        P.wait(t, "v", pv_v2)
                        P.T(t.matmul(ps_bb[:, 0:CH], ones_r[0:1, :],
                                     brow1[0:1, 0:CH], start=True, stop=True))
                        t_bb[0] = P.T(t.matmul(ps_cb[:, 0:CH], ones_r[0:1, :],
                                               crow1[0:1, 0:CH], start=True,
                                               stop=True))

                    for lt in range(NT_C):
                        dsl = dab16[:, (lt % 2) * CH:(lt % 2 + 1) * CH]
                        pv_v3 = P.cnt["v"]
                        s_da = [0]

                        @blk.scalar
                        def _(s):
                            P.wait(s, "v", pv_v3)
                            s_da[0] = P.S(s.activation(
                                dsl, dtb_[:, lt * CH:(lt + 1) * CH], AF.Exp,
                                scale=asb[:, si * NT_C + lt:si * NT_C + lt + 1]))

                        @blk.vector
                        def _(v):
                            P.wait(v, "s", s_da[0])
                            P.wait(v, "t", t_bb[0])
                            P.wait(v, "v", v_w[0])
                            P.V(v.tensor_tensor(d1b[:],
                                                wbuf[:, lt * CH:(lt + 1) * CH],
                                                ps_bb[:, 0:CH], ALU.mult))
                            n_sc = P.V(v.tensor_tensor_scan(
                                hsc[:], dsl, d1b[:],
                                hin[:, lt * STATE + si:lt * STATE + si + 1],
                                ALU.mult, ALU.add))
                            P.wait(v, "v", n_sc)
                            P.V(v.tensor_copy(
                                hin[:, lt * STATE + si:lt * STATE + si + 1],
                                hsc[:, CH - 1:CH]))
                            P.V(v.tensor_tensor(ytmp[:], hsc[:], ps_cb[:, 0:CH],
                                                ALU.mult))
                            yb = ybuf[:, lt * CH:(lt + 1) * CH]
                            P.V(v.tensor_tensor(yb, yb, ytmp[:], ALU.add))

                for lt in range(NT_C):
                    dy_z = [0]

                    @blk.sync
                    def _(sy):
                        pv_s = P.cnt["s"]
                        P.wait(sy, "s", pv_s)
                        P.dma(sy, "dy", zbuf[:, 0:CH],
                              I["zi"][lt * 128:(lt + 1) * 128, c0:c0 + CH])
                        dy_z[0] = P.cnt["dy"]

                    s_z = [0]

                    @blk.scalar
                    def _(s):
                        P.wait(s, "dy", dy_z[0])
                        P.S(s.activation(szb[:], zbuf[:, 0:CH], AF.Exp, scale=-1.0))
                        P.S(s.activation(szb[:], szb[:], AF.Ln, bias=1.0))
                        s_z[0] = P.S(s.activation(szb[:], szb[:], AF.Exp, scale=-1.0))

                    @blk.vector
                    def _(v):
                        P.wait(v, "s", s_z[0])
                        P.V(v.tensor_tensor(szb[:], szb[:], zbuf[:, 0:CH], ALU.mult))
                        yb = ybuf[:, lt * CH:(lt + 1) * CH]
                        P.V(v.scalar_tensor_tensor(yb, xcb[:, lt * CH:(lt + 1) * CH],
                                                   dcsb[:, lt:lt + 1], yb,
                                                   ALU.mult, ALU.add))
                        P.V(v.tensor_tensor(ybf[:, lt * CH:(lt + 1) * CH],
                                            yb, szb[:], ALU.mult))
                v_y = P.cnt["v"]
                if DEBUG2:
                    @blk.sync
                    def _(sy):
                        P.wait(sy, "v", v_y)
                        for lt in range(NT_C):
                            P.dma(sy, "dy",
                                  O["dbg_y"][lt * 128:(lt + 1) * 128, c0:c0 + CH],
                                  ybf[:, lt * CH:(lt + 1) * CH])

                for dot in range(NT_D):
                    ps = ps_s1 if dot % 2 == 0 else ps_s1b
                    pv_v9 = P.cnt["v"]
                    t_o = [0]

                    @blk.tensor
                    def _(t):
                        P.wait(t, "v", v_y)
                        P.wait(t, "v", pv_v9)
                        if ci == 0 and dot == 0:
                            P.wait(t, "dg", dg_ow)
                        for kt in range(NT_C):
                            inst = t.matmul(
                                ps[:, 0:CH],
                                owsb[:, (dot * NT_C + kt) * 128:(dot * NT_C + kt + 1) * 128],
                                ybf[:, kt * CH:(kt + 1) * CH],
                                start=(kt == 0), stop=(kt == NT_C - 1))
                        t_o[0] = P.T(inst)

                    @blk.vector
                    def _(v):
                        P.wait(v, "t", t_o[0])
                        P.V(v.tensor_scalar(
                            s1_sb[:, dot * TOWN + c0:dot * TOWN + c0 + CH],
                            ps[:, 0:CH], owbsb[:, dot:dot + 1], None, ALU.add))

        if DEBUG2:
            pv_vD = P.cnt["v"]

            @blk.sync
            def _(sy):
                P.wait(sy, "v", pv_vD)
                for i in range(NT_D):
                    P.dma(sy, "dy", O["dbg_s1"][i * 128:(i + 1) * 128, :],
                          s1_sb[:, i * TOWN:(i + 1) * TOWN])

        fence(nc, blk, P)
        with contextlib.ExitStack() as _sc:
            wtile = _sc.enter_context(nc.sbuf_tensor("wtile", [128, FFN], F32))
            qtile = _sc.enter_context(nc.sbuf_tensor("qtile", [128, FFN], BF16))
            sc = _sc.enter_context(nc.sbuf_tensor("sc", [128, 3 * (FFN // 16)], F32))
            tb = _sc.enter_context(nc.sbuf_tensor("tb", [128, FFN], F32))
            ab = _sc.enter_context(nc.sbuf_tensor("ab", [128, FFN], F32))
            ub2 = _sc.enter_context(nc.sbuf_tensor("ub2", [128, FFN], F32))
            qb = _sc.enter_context(nc.sbuf_tensor("qb", [128, FFN], F32))
            ttile = _sc.enter_context(nc.sbuf_tensor("ttile", [128, 128], BF16))
            ps_t = _sc.enter_context(nc.psum_tensor("ps_t", [128, 128], BF16))

            def quant_rows(src_d, n_rows_t, width, dst_d):
                for rt in range(n_rows_t):
                    pv_v = P.cnt["v"]
                    pv_s = P.cnt["s"]
                    dy_w = [0]

                    @blk.sync
                    def _(sy):
                        P.wait(sy, "v", pv_v)
                        P.wait(sy, "s", pv_s)
                        P.dma(sy, "dy", wtile[:, 0:width],
                              src_d[rt * 128:(rt + 1) * 128, :])
                        dy_w[0] = P.cnt["dy"]

                    emit_fp4(nc, blk, P, wtile[:, 0:width], qtile[:, 0:width],
                             width, sc, tb, ab, ub2, qb, cvc[:, 1:2],
                             wait_map={"dy": dy_w[0], "t": P.cnt["t"]})
                    v_q = P.cnt["v"]

                    for ct in range(width // 128):
                        t_t = [0]
                        pv_s2 = P.cnt["s"]

                        @blk.tensor
                        def _(t):
                            P.wait(t, "dg", d0g)
                            P.wait(t, "v", v_q)
                            P.wait(t, "s", pv_s2)
                            t_t[0] = P.T(t.transpose(
                                ps_t[:, :], qtile[:, ct * 128:(ct + 1) * 128],
                                idenb[:]))

                        s_t = [0]

                        pv_dyt = P.cnt["dy"]

                        @blk.scalar
                        def _(s):
                            P.wait(s, "t", t_t[0])
                            P.wait(s, "dy", pv_dyt)
                            s_t[0] = P.S(s.activation(ttile[:], ps_t[:, :], AF.Copy))

                        @blk.sync
                        def _(sy):
                            P.wait(sy, "s", s_t[0])
                            P.dma(sy, "dy", dst_d[rt][:, ct * 128:(ct + 1) * 128],
                                  ttile[:])

            @blk.vector
            def _(v):
                P.wait(v, "dg", d0g)

            quant_rows(I["wg"], NT_FF, D, wgq_d)
            quant_rows(I["wu"], NT_FF, D, wuq_d)
            quant_rows(I["wd"], NT_D, FFN, wdq_d)
        dy_wq = P.cnt["dy"]

        if DEBUG2:
            @blk.sync
            def _(sy):
                P.wait(sy, "dy", P.cnt["dy"])
                for i in range(NT_FF):
                    P.dma(sy, "dy", O["dbg_wgq"][i], wgq_d[i])

        fence(nc, blk, P)
        CHD = 256
        NTT = CHD // 128
        with contextlib.ExitStack() as _sc:
            x1b = _sc.enter_context(nc.sbuf_tensor("x1b", [128, NT_D * CHD], F32))
            xnf = _sc.enter_context(nc.sbuf_tensor("xnf", [128, NT_D * CHD], F32))
            xn_tm = _sc.enter_context(nc.sbuf_tensor("xn_tm", [128, NTT * D], F32))
            xq_tm = _sc.enter_context(nc.sbuf_tensor("xq_tm", [128, NTT * D], BF16))
            xqf = _sc.enter_context(nc.sbuf_tensor("xqf", [128, NT_D * CHD], BF16))
            sc2 = _sc.enter_context(nc.sbuf_tensor("sc2", [128, 3 * (FFN // 16)], F32))
            tb2 = _sc.enter_context(nc.sbuf_tensor("tb2", [128, FFN], F32))
            ab2 = _sc.enter_context(nc.sbuf_tensor("ab2", [128, FFN], F32))
            ub3 = _sc.enter_context(nc.sbuf_tensor("ub3", [128, FFN], F32))
            qb2 = _sc.enter_context(nc.sbuf_tensor("qb2", [128, FFN], F32))
            wqb = _sc.enter_context(nc.sbuf_tensor("wqb", [128, 2 * 8 * 128], BF16))
            sgb = _sc.enter_context(nc.sbuf_tensor("sgb", [128, CHD], BF16))
            sgt = _sc.enter_context(nc.sbuf_tensor("sgt", [128, CHD], F32))
            upb2 = _sc.enter_context(nc.sbuf_tensor("upb2", [128, CHD], F32))
            upb = _sc.enter_context(nc.sbuf_tensor("upb", [128, CHD], BF16))
            hfp = _sc.enter_context(nc.sbuf_tensor("hfp", [128, CHD], F32))
            h_tm = _sc.enter_context(nc.sbuf_tensor("h_tm", [128, NTT * FFN], F32))
            hq_tm = _sc.enter_context(nc.sbuf_tensor("hq_tm", [128, NTT * FFN], BF16))
            hqf = _sc.enter_context(nc.sbuf_tensor("hqf", [128, NT_FF * CHD], BF16))
            g1b = _sc.enter_context(nc.sbuf_tensor("g1b", [128, CHD], BF16))
            x2b = _sc.enter_context(nc.sbuf_tensor("x2b", [128, CHD], F32))
            ps_g = _sc.enter_context(nc.psum_tensor("ps_g", [128, CHD], F32))
            ps_g2 = _sc.enter_context(nc.psum_tensor("ps_g2", [128, CHD], F32))
            ps_tt = _sc.enter_context(nc.psum_tensor("ps_tt", [128, 512], F32))
            ps_tb = _sc.enter_context(nc.psum_tensor("ps_tb", [128, 512], BF16))

            for ci in range(TOWN // CHD):
                c0 = ci * CHD
                pv_v = P.cnt["v"]
                pv_s0 = P.cnt["s"]
                pv_t0 = P.cnt["t"]
                dy_x = [0]

                @blk.sync
                def _(sy):
                    P.wait(sy, "v", pv_v)
                    P.wait(sy, "s", pv_s0)
                    for i in range(NT_D):
                        P.dma(sy, "dy", x1b[:, i * CHD:(i + 1) * CHD],
                              I["x1o"][i * 128:(i + 1) * 128, c0:c0 + CHD])
                        P.dma(sy, "dy", xnf[:, i * CHD:(i + 1) * CHD],
                              I["xn1f"][i * 128:(i + 1) * 128, c0:c0 + CHD])
                    dy_x[0] = P.cnt["dy"]

                for tt in range(NTT):
                    for i4 in range(NT_D // 4):
                        pv_s = P.cnt["s"]
                        t_tt = [0]

                        @blk.tensor
                        def _(t):
                            P.wait(t, "dy", dy_x[0])
                            P.wait(t, "s", pv_s)
                            for j in range(4):
                                i = i4 * 4 + j
                                t_tt[0] = P.T(t.transpose(
                                    ps_tt[:, j * 128:(j + 1) * 128],
                                    xnf[:, i * CHD + tt * 128:i * CHD + (tt + 1) * 128],
                                    iden[:]))

                        @blk.scalar
                        def _(s):
                            P.wait(s, "t", t_tt[0])
                            if tt == 0 and i4 == 0:
                                P.wait(s, "v", pv_v)
                            P.S(s.activation(
                                xn_tm[:, tt * D + i4 * 512:tt * D + (i4 + 1) * 512],
                                ps_tt[:, 0:512], AF.Copy))
                s_tm = P.cnt["s"]

                for tt in range(NTT):
                    emit_fp4(nc, blk, P, xn_tm[:, tt * D:(tt + 1) * D],
                             xq_tm[:, tt * D:(tt + 1) * D], D, sc2, tb2, ab2,
                             ub3, qb2, wait_map={"s": s_tm, "t": pv_t0})
                v_q = P.cnt["v"]

                for tt in range(NTT):
                    for i4 in range(NT_D // 4):
                        pv_s = P.cnt["s"]
                        t_tb = [0]

                        @blk.tensor
                        def _(t):
                            P.wait(t, "v", v_q)
                            P.wait(t, "s", pv_s)
                            for j in range(4):
                                i = i4 * 4 + j
                                t_tb[0] = P.T(t.transpose(
                                    ps_tb[:, j * 128:(j + 1) * 128],
                                    xq_tm[:, tt * D + i * 128:tt * D + (i + 1) * 128],
                                    idenb[:]))

                        @blk.scalar
                        def _(s):
                            P.wait(s, "t", t_tb[0])
                            if tt == 0 and i4 == 0:
                                P.wait(s, "t", pv_t0)
                            for j in range(4):
                                i = i4 * 4 + j
                                P.S(s.activation(
                                    xqf[:, i * CHD + tt * 128:i * CHD + (tt + 1) * 128],
                                    ps_tb[:, j * 128:(j + 1) * 128], AF.Copy))
                s_xq = P.cnt["s"]
                if DEBUG2:
                    @blk.sync
                    def _(sy):
                        P.wait(sy, "s", s_xq)
                        for i in range(NT_D):
                            P.dma(sy, "dy",
                                  O["dbg_xqf"][i * 128:(i + 1) * 128, c0:c0 + CHD],
                                  xqf[:, i * CHD:(i + 1) * CHD])

                for ft in range(NT_FF):
                    par = ft % 2
                    wsl = wqb[:, par * 8 * 128:(par + 1) * 8 * 128]
                    for which, src_d in ((0, wgq_d), (1, wuq_d)):
                        pv_t = P.cnt["t"]

                        @blk.sync
                        def _(sy):
                            P.wait(sy, "t", pv_t)
                            if ci == 0 and ft == 0 and which == 0:
                                P.wait(sy, "dy", dy_wq)
                            P.dma(sy, "dy", wsl, src_d[ft])
                        dy_wg = P.cnt["dy"]
                        ps = ps_g if which == 0 else ps_g2
                        pv_s3 = P.cnt["s"]
                        t_gu = [0]

                        @blk.tensor
                        def _(t):
                            P.wait(t, "dy", dy_wg)
                            P.wait(t, "s", s_xq)
                            P.wait(t, "s", pv_s3)
                            for kt in range(NT_D):
                                inst = t.matmul(ps[:, 0:CHD],
                                                wsl[:, kt * 128:(kt + 1) * 128],
                                                xqf[:, kt * CHD:(kt + 1) * CHD],
                                                start=(kt == 0), stop=(kt == NT_D - 1))
                            t_gu[0] = P.T(inst)

                        @blk.scalar
                        def _(s):
                            P.wait(s, "t", t_gu[0])
                            if which == 0:
                                pv_vh = P.cnt["v"]
                                P.wait(s, "v", pv_vh)
                                P.S(s.activation(sgt[:], ps[:, 0:CHD], AF.Exp,
                                                 scale=-1.0))
                                P.S(s.activation(sgt[:], sgt[:], AF.Ln, bias=1.0))
                                P.S(s.activation(sgt[:], sgt[:], AF.Exp, scale=-1.0))
                                P.S(s.activation(upb2[:], ps[:, 0:CHD], AF.Copy))
                            else:
                                P.S(s.activation(upb[:], ps[:, 0:CHD], AF.Copy))
                    s_gu = P.cnt["s"]
                    v_h = [0]

                    pv_thf = P.cnt["t"]

                    @blk.vector
                    def _(v):
                        P.wait(v, "s", s_gu)
                        P.wait(v, "t", pv_thf)
                        P.V(v.tensor_tensor(hfp[:], sgt[:], upb2[:], ALU.mult))
                        P.V(v.tensor_tensor(hfp[:], hfp[:], upb[:], ALU.mult))
                        v_h[0] = P.cnt["v"]

                    pv_s4 = P.cnt["s"]
                    t_h = [0]

                    @blk.tensor
                    def _(t):
                        P.wait(t, "v", v_h[0])
                        P.wait(t, "s", pv_s4)
                        for tt in range(NTT):
                            t_h[0] = P.T(t.transpose(
                                ps_tt[:, tt * 128:(tt + 1) * 128],
                                hfp[:, tt * 128:(tt + 1) * 128],
                                iden[:]))

                    @blk.scalar
                    def _(s):
                        P.wait(s, "t", t_h[0])
                        if ft == 0:
                            P.wait(s, "v", pv_v)
                        for tt in range(NTT):
                            P.S(s.activation(
                                h_tm[:, tt * FFN + ft * 128:tt * FFN + (ft + 1) * 128],
                                ps_tt[:, tt * 128:(tt + 1) * 128], AF.Copy))
                s_htm = P.cnt["s"]
                pv_t1 = P.cnt["t"]

                for tt in range(NTT):
                    emit_fp4(nc, blk, P, h_tm[:, tt * FFN:(tt + 1) * FFN],
                             hq_tm[:, tt * FFN:(tt + 1) * FFN], FFN, sc2, tb2,
                             ab2, ub3, qb2, wait_map={"s": s_htm, "t": pv_t1})
                v_hq = P.cnt["v"]

                for f2 in range(NT_FF // 2):
                    pv_s5 = P.cnt["s"]
                    t_hb = [0]

                    @blk.tensor
                    def _(t):
                        P.wait(t, "v", v_hq)
                        P.wait(t, "s", pv_s5)
                        for j in range(2):
                            ft = f2 * 2 + j
                            for tt in range(NTT):
                                t_hb[0] = P.T(t.transpose(
                                    ps_tb[:, (j * NTT + tt) * 128:(j * NTT + tt + 1) * 128],
                                    hq_tm[:, tt * FFN + ft * 128:tt * FFN + (ft + 1) * 128],
                                    idenb[:]))

                    @blk.scalar
                    def _(s):
                        P.wait(s, "t", t_hb[0])
                        if f2 == 0:
                            P.wait(s, "t", pv_t1)
                        for j in range(2):
                            ft = f2 * 2 + j
                            for tt in range(NTT):
                                P.S(s.activation(
                                    hqf[:, ft * CHD + tt * 128:ft * CHD + (tt + 1) * 128],
                                    ps_tb[:, (j * NTT + tt) * 128:(j * NTT + tt + 1) * 128],
                                    AF.Copy))
                s_hq = P.cnt["s"]
                if DEBUG2:
                    @blk.sync
                    def _(sy):
                        P.wait(sy, "s", s_hq)
                        for i in range(NT_FF):
                            P.dma(sy, "dy",
                                  O["dbg_hqf"][i * 128:(i + 1) * 128, c0:c0 + CHD],
                                  hqf[:, i * CHD:(i + 1) * CHD])

                for dot in range(NT_D):
                    par = dot % 2
                    wsl = wqb[:, par * 8 * 128:(par + 1) * 8 * 128]
                    ps = ps_g if dot % 2 == 0 else ps_g2
                    pv_s6 = P.cnt["s"]
                    t_dn = [0]
                    for grp in range(3):
                        ksl = list(range(grp * 8, min((grp + 1) * 8, NT_FF)))
                        pv_t = P.cnt["t"]

                        @blk.sync
                        def _(sy):
                            P.wait(sy, "t", pv_t)
                            P.dma(sy, "dy", wsl[:, 0:len(ksl) * 128],
                                  wdq_d[dot][:, ksl[0] * 128:(ksl[-1] + 1) * 128])
                        dy_wd = P.cnt["dy"]

                        @blk.tensor
                        def _(t):
                            P.wait(t, "dy", dy_wd)
                            if grp == 0:
                                P.wait(t, "s", s_hq)
                                P.wait(t, "s", pv_s6)
                            for j, kt in enumerate(ksl):
                                inst = t.matmul(ps[:, 0:CHD],
                                                wsl[:, j * 128:(j + 1) * 128],
                                                hqf[:, kt * CHD:(kt + 1) * CHD],
                                                start=(kt == 0),
                                                stop=(kt == NT_FF - 1))
                            t_dn[0] = P.T(inst)

                    s_g1 = [0]

                    @blk.scalar
                    def _(s):
                        P.wait(s, "t", t_dn[0])
                        pv_vg = P.cnt["v"]
                        P.wait(s, "v", pv_vg)
                        P.S(s.activation(sgt[:], ps[:, 0:CHD], AF.Exp, scale=-1.0))
                        P.S(s.activation(sgt[:], sgt[:], AF.Ln, bias=1.0))
                        P.S(s.activation(sgt[:], sgt[:], AF.Exp, scale=-1.0))
                        s_g1[0] = P.S(s.activation(upb2[:], ps[:, 0:CHD], AF.Copy))

                    v_x2 = [0]

                    pv_dyx = P.cnt["dy"]

                    @blk.vector
                    def _(v):
                        P.wait(v, "s", s_g1[0])
                        P.wait(v, "dy", pv_dyx)
                        P.V(v.tensor_tensor(g1b[:], sgt[:], upb2[:], ALU.mult))
                        P.V(v.tensor_tensor(x2b[:],
                                            s1_sb[:, dot * TOWN + c0:dot * TOWN + c0 + CHD],
                                            g1b[:], ALU.mult))
                        P.V(v.tensor_tensor(x2b[:], x2b[:],
                                            x1b[:, dot * CHD:(dot + 1) * CHD],
                                            ALU.add))
                        v_x2[0] = P.cnt["v"]

                    @blk.sync
                    def _(sy):
                        P.wait(sy, "v", v_x2[0])
                        P.dma(sy, "dy", O["x2"][dot * 128:(dot + 1) * 128, c0:c0 + CHD],
                              x2b[:])

        @blk.sync
        def _(sy):
            P.wait(sy, "dy", P.cnt["dy"])

        @blk.gpsimd
        def _(g):
            P.wait(g, "dg", P.cnt["dg"])
            P.wait(g, "dy", P.cnt["dy"])

    return nc



_CACHE = {}


def kernel(**inputs):
    x = np.asarray(inputs["L0_attn_x"], np.float32)
    H = host_prepare(inputs)

    if "nc1" not in _CACHE:
        _CACHE["nc1"] = build_launch1()
        _CACHE["nc2"] = build_launch2()
    nc1, nc2 = _CACHE["nc1"], _CACHE["nc2"]

    H["cvals"] = np.ascontiguousarray(
        np.stack([np.full(128, EPS, np.float32),
                  np.full(128, MAGIC, np.float32)], 1))
    shared1 = {k: H[k] for k in
               ("idenf", "cvals", "wfT", "woT", "centT", "gcol", "bcol", "b1col", "b2w",
                "w1T", "w2T", "ipT", "ipb", "cw", "cbc", "dtT", "dtb", "alog",
                "bwT", "cwTT", "bbr", "cbr")}
    in_maps1 = []
    for c in range(8):
        b, q = divmod(c, 4)
        start = q * TOWN
        xe = np.zeros((EXT, D), np.float32)
        lo = start - WARM
        src_lo = max(lo, 0)
        xe[src_lo - lo:] = x[b, src_lo:start + TOWN]
        m = dict(shared1)
        m["x_ext"] = np.ascontiguousarray(xe.T)
        m["m3col"] = np.full((128, 1), 0.0 if q == 0 else 1.0, np.float32)
        in_maps1.append(m)

    res1 = run_bass_kernel_spmd(nc1, in_maps1, core_ids=list(range(8)))
    r1 = res1.results

    shared2 = {k: H[k] for k in ("alog", "dcol", "owT", "owb", "wg", "wu",
                                 "wd", "idenf", "cvals")}
    in_maps2 = []
    for c in range(8):
        b, q = divmod(c, 4)
        m = dict(shared2)
        m["x1o"] = np.ascontiguousarray(
            np.asarray(r1[c]["x1"])[:, WARM:])
        m["xn1f"] = np.asarray(r1[c]["xn1f"])
        m["xci"] = np.asarray(r1[c]["xc"])
        m["dti"] = np.asarray(r1[c]["dt"])
        m["zi"] = np.asarray(r1[c]["z"])
        m["bmi"] = np.asarray(r1[c]["bm"])
        m["cmi"] = np.asarray(r1[c]["cm"])
        m["summs"] = np.ascontiguousarray(
            np.stack([np.asarray(r1[4 * b + r]["summ"]) for r in range(4)]))
        sel = np.zeros((128, 4), np.float32)
        sel[:, :q] = 1.0
        m["selc"] = sel
        in_maps2.append(m)

    res2 = run_bass_kernel_spmd(nc2, in_maps2, core_ids=list(range(8)))
    out = np.empty((2, 8192, D), np.float32)
    for c in range(8):
        b, q = divmod(c, 4)
        out[b, q * TOWN:(q + 1) * TOWN] = np.asarray(res2.results[c]["x2"]).T
    return out


# version 12
# speedup vs baseline: 1.1644x; 1.1644x over previous
import contextlib
import numpy as np
import ml_dtypes
import concourse.bass as bass
import concourse.mybir as mybir
from concourse.bass_utils import run_bass_kernel_spmd

F32 = mybir.dt.float32
BF16 = mybir.dt.bfloat16
AF = mybir.ActivationFunctionType
ALU = mybir.AluOpType
AX = mybir.AxisListType

TOWN = 2048
WARM = 512
EXT = WARM + TOWN
NCH = EXT // 512
CH = 512
D = 1024
HID = 2048
INNER = 2048
STATE = 16
FMOE = 4096
NE = 8
FFN = 2816
EPS = 1e-5
MAGIC = float(np.float32(2.0 ** 23))

NT_D = D // 128
NT_H = HID // 128
NT_F = FMOE // 128
NT_C = INNER // 128
NT_FF = FFN // 128

BF = ml_dtypes.bfloat16
DEBUG1 = False
DEBUG2 = False


class Prog:
    def __init__(self, nc, st):
        self.nc = nc
        self.sem = {}
        self.cnt = {}
        for name in ("v", "s", "t", "g", "dg", "dy"):
            self.sem[name] = st.enter_context(nc.semaphore("sem_" + name))
            self.cnt[name] = 0

    def _mark(self, q, inst, n=1):
        self.cnt[q] += n
        inst.then_inc(self.sem[q], n)
        return self.cnt[q]

    def V(self, inst):
        return self._mark("v", inst)

    def S(self, inst):
        return self._mark("s", inst)

    def T(self, inst):
        return self._mark("t", inst)

    def G(self, inst):
        return self._mark("g", inst)

    def dma(self, eng, q, out, in_):
        inst = eng.dma_start(out=out, in_=in_)
        return self._mark(q, inst, 16)

    def wait(self, eng, q, val):
        assert val <= self.cnt[q], (q, val, self.cnt[q])
        if val > 0:
            eng.wait_ge(self.sem[q], val)

    def snap(self):
        return dict(self.cnt)



def _tiles_lhsT(w):
    n, k = w.shape
    t = w.reshape(n // 128, 128, k // 128, 128)
    return np.ascontiguousarray(t.transpose(0, 3, 2, 1).reshape(n // 128, 128, k))


def _colmaj(vec):
    return np.ascontiguousarray(vec.reshape(-1, 128).T)


def host_prepare(w):
    H = {}
    H["wfT"] = _tiles_lhsT(w["L0_attn_W_fwd"])
    H["woT"] = _tiles_lhsT(w["L0_attn_W_out"])
    H["centT"] = np.ascontiguousarray(w["L0_ffn_centroids"].T)
    H["gcol"] = np.ascontiguousarray(
        w["L0_ffn_gamma"].reshape(NE, NT_D, 128).transpose(2, 0, 1).reshape(128, NE * NT_D))
    H["bcol"] = np.ascontiguousarray(
        w["L0_ffn_beta"].reshape(NE, NT_D, 128).transpose(2, 0, 1).reshape(128, NE * NT_D))
    H["b1col"] = np.ascontiguousarray(
        w["L0_ffn_b1"].reshape(NE, NT_F, 128).transpose(2, 0, 1).reshape(128, NE * NT_F))
    H["b2w"] = np.ascontiguousarray(w["L0_ffn_b2"])
    H["w1T"] = np.ascontiguousarray(
        np.stack([_tiles_lhsT(w["L0_ffn_W1"][e]) for e in range(NE)]))
    H["w2T"] = np.ascontiguousarray(
        np.stack([_tiles_lhsT(w["L0_ffn_W2"][e]) for e in range(NE)]))
    H["ipT"] = _tiles_lhsT(w["L1_attn_in_proj_w"])
    H["ipb"] = _colmaj(w["L1_attn_in_proj_b"])
    H["cw"] = np.ascontiguousarray(
        w["L1_attn_conv_w"][:, 0, :].reshape(NT_C, 128, 4).transpose(1, 0, 2).reshape(128, 4 * NT_C))
    H["cbc"] = _colmaj(w["L1_attn_conv_b"])
    H["dtT"] = _tiles_lhsT(w["L1_attn_dt_w"])
    H["dtb"] = _colmaj(w["L1_attn_dt_b"])
    H["alog"] = np.ascontiguousarray(
        w["L1_attn_A_log"].reshape(NT_C, 128, STATE).transpose(1, 2, 0).reshape(128, STATE * NT_C))
    H["bwT"] = np.ascontiguousarray(w["L1_attn_B_w"].T.reshape(NT_C, 128, 16))
    H["cwTT"] = np.ascontiguousarray(w["L1_attn_C_w"].T.reshape(NT_C, 128, 16))
    H["bbr"] = np.ascontiguousarray(w["L1_attn_B_b"].reshape(16, 1))
    H["cbr"] = np.ascontiguousarray(w["L1_attn_C_b"].reshape(16, 1))
    H["dcol"] = _colmaj(w["L1_attn_D"])
    H["owT"] = _tiles_lhsT(w["L1_attn_out_w"])
    H["owb"] = _colmaj(w["L1_attn_out_b"])
    H["wg"] = np.ascontiguousarray(w["L1_ffn_w_gate"])
    H["wu"] = np.ascontiguousarray(w["L1_ffn_w_up"])
    H["wd"] = np.ascontiguousarray(w["L1_ffn_w_down"])
    H["idenf"] = np.eye(128, dtype=np.float32)
    return H



def emit_ln(nc, blk, P, src, w, rows, sqb, ps_s, ps_q, ps_m, ps_r, ones_c,
            ones_r, wait_map=None, ones_sq=None, eps_col=None):
    if ones_sq is None:
        ones_sq = ones_c
    mean = rows[0:1, 0:w]
    var = rows[0:1, CH:CH + w]
    tmp = rows[0:1, 2 * CH:2 * CH + w]

    @blk.tensor
    def _(t):
        if wait_map:
            for q, val in wait_map.items():
                P.wait(t, q, val)
        for i in range(NT_D):
            inst = t.matmul(ps_s[0:1, 0:w], ones_c, src(i),
                            start=(i == 0), stop=(i == NT_D - 1))
        P.T(inst)

    for i in range(NT_D):
        sq_done = [0]
        pv_t = P.cnt["t"]

        @blk.scalar
        def _(s):
            if wait_map and i == 0:
                for q, val in wait_map.items():
                    P.wait(s, q, val)
            P.wait(s, "t", pv_t)
            sq_done[0] = P.S(s.activation(sqb[:, 0:w], src(i), AF.Square))

        @blk.tensor
        def _(t):
            P.wait(t, "s", sq_done[0])
            inst = t.matmul(ps_q[0:1, 0:w], ones_sq, sqb[:, 0:w],
                            start=(i == 0), stop=(i == NT_D - 1))
            if i == NT_D - 1:
                P.T(inst)
    t_s2 = P.cnt["t"]

    v_var = [0]

    @blk.vector
    def _(v):
        P.wait(v, "t", t_s2)
        P.V(v.tensor_scalar(mean, ps_s[0:1, 0:w], 1.0 / D, None, ALU.mult))
        P.V(v.tensor_scalar(var, ps_q[0:1, 0:w], 1.0 / D, None, ALU.mult))
        P.V(v.tensor_tensor(tmp, mean, mean, ALU.mult))
        P.V(v.tensor_tensor(var, var, tmp, ALU.subtract))
        v_var[0] = P.cnt["v"]

    s_r = [0]

    @blk.scalar
    def _(s):
        P.wait(s, "v", v_var[0])
        P.S(s.activation(tmp, var, AF.Ln, bias=eps_col[0:1, 0:1]))
        s_r[0] = P.S(s.activation(tmp, tmp, AF.Exp, scale=-0.5))

    @blk.tensor
    def _(t):
        P.wait(t, "s", s_r[0])
        P.T(t.matmul(ps_m[:, 0:w], ones_r, mean, start=True, stop=True))
        P.T(t.matmul(ps_r[:, 0:w], ones_r, tmp, start=True, stop=True))
    return P.cnt["t"]


def emit_fp4(nc, blk, P, x, out, w, sc, tb, ab, ub_, qb, magic_col=None,
             wait_map=None):
    G = w // 16
    bmax = sc[:, 0:G]
    scale = sc[:, G:2 * G]
    rinv = sc[:, 2 * G:3 * G]
    x3 = x.rearrange("p (g k) -> p g k", k=16)

    v1 = [0]

    @blk.vector
    def _(v):
        if wait_map:
            for q_, val in wait_map.items():
                P.wait(v, q_, val)
        n_rd = P.V(v.tensor_reduce(bmax, x3, AX.X, ALU.max,
                                   apply_absolute_value=True))
        P.wait(v, "v", n_rd)
        P.V(v.tensor_scalar(scale, bmax, 1.0 / 6.0, 1e-12, ALU.mult, ALU.max))
        v1[0] = P.cnt["v"]

    s1 = [0]

    @blk.scalar
    def _(s):
        P.wait(s, "v", v1[0])
        P.S(s.activation(rinv, scale, AF.Ln))
        P.S(s.activation(rinv, rinv, AF.Exp, scale=-1.0))
        P.S(s.activation(ab[:, 0:w], x, AF.Abs))
        s1[0] = P.S(s.activation(ub_[:, 0:w], x, AF.Sign))

    @blk.vector
    def _(v):
        P.wait(v, "s", s1[0])
        rb = rinv.unsqueeze(2).broadcast_to([128, G, 16])
        P.V(v.tensor_tensor(tb[:, 0:w].rearrange("p (g k) -> p g k", k=16),
                            ab[:, 0:w].rearrange("p (g k) -> p g k", k=16),
                            rb, ALU.mult))
        P.V(v.tensor_scalar(tb[:, 0:w], tb[:, 0:w], 6.0, None, ALU.min))
        P.V(v.tensor_scalar(ab[:, 0:w], tb[:, 0:w], 2.0, MAGIC, ALU.mult, ALU.add))
        P.V(v.tensor_scalar(qb[:, 0:w], ab[:, 0:w], MAGIC, 0.5, ALU.subtract,
                            ALU.mult))
        P.V(v.tensor_scalar(qb[:, 0:w], qb[:, 0:w], 2.0, None, ALU.min))
        P.V(v.tensor_scalar(ab[:, 0:w], tb[:, 0:w], MAGIC, None, ALU.add))
        P.V(v.tensor_scalar(ab[:, 0:w], ab[:, 0:w], MAGIC + 2.0, 0.0,
                            ALU.subtract, ALU.max))
        P.V(v.scalar_tensor_tensor(qb[:, 0:w], ab[:, 0:w], 2.0, qb[:, 0:w],
                                   ALU.min, ALU.add))
        P.V(v.tensor_scalar(ab[:, 0:w], tb[:, 0:w], 0.5, MAGIC, ALU.mult, ALU.add))
        P.V(v.tensor_scalar(ab[:, 0:w], ab[:, 0:w], MAGIC + 2.0, 2.0,
                            ALU.subtract, ALU.mult))
        P.V(v.scalar_tensor_tensor(qb[:, 0:w], ab[:, 0:w], 0.0, qb[:, 0:w],
                                   ALU.max, ALU.add))
        P.V(v.tensor_tensor(qb[:, 0:w], qb[:, 0:w], ub_[:, 0:w], ALU.mult))
        sb = scale.unsqueeze(2).broadcast_to([128, G, 16])
        P.V(v.tensor_tensor(out.rearrange("p (g k) -> p g k", k=16),
                            qb[:, 0:w].rearrange("p (g k) -> p g k", k=16),
                            sb, ALU.mult))
    return P.cnt["v"]


def fence(nc, blk, P):
    snap = P.snap()

    for eng_name in ("vector", "scalar", "tensor", "gpsimd", "sync"):
        def _body(e, _snap=snap):
            for q, val in _snap.items():
                P.wait(e, q, val)
        getattr(blk, eng_name)(_body)



def build_launch1():
    nc = bass.Bass("TRN2", target_bir_lowering=False)
    I, O = {}, {}

    def di(name, shape, dt=F32):
        I[name] = nc.dram_tensor(name, list(shape), dt, kind="ExternalInput").ap()

    def do_(name, shape, dt=F32):
        O[name] = nc.dram_tensor(name, list(shape), dt, kind="ExternalOutput").ap()

    di("x_ext", [D, EXT])
    di("m3col", [128, 1])
    di("idenf", [128, 128])
    di("cvals", [128, 2])
    di("wfT", [3 * NT_H, 128, D])
    di("woT", [NT_D, 128, HID])
    di("centT", [D, NE])
    di("gcol", [128, NE * NT_D])
    di("bcol", [128, NE * NT_D])
    di("b1col", [128, NE * NT_F])
    di("b2w", [NE, D])
    di("w1T", [NE, NT_F, 128, D])
    di("w2T", [NE, NT_D, 128, FMOE])
    di("ipT", [32, 128, D])
    di("ipb", [128, 32])
    di("cw", [128, 4 * NT_C])
    di("cbc", [128, NT_C])
    di("dtT", [NT_C, 128, INNER])
    di("dtb", [128, NT_C])
    di("alog", [128, STATE * NT_C])
    di("bwT", [NT_C, 128, 16])
    di("cwTT", [NT_C, 128, 16])
    di("bbr", [16, 1])
    di("cbr", [16, 1])

    do_("x1", [D, EXT])
    if DEBUG1:
        do_("dbg_xn", [D, EXT], BF16)
        do_("dbg_s", [D, EXT], BF16)
        do_("dbg_xhat", [D, EXT], BF16)
        do_("dbg_mrow", [NE, EXT])
        do_("dbg_acc", [D, EXT], BF16)
    do_("xn1f", [D, TOWN])
    do_("xc", [INNER, TOWN], BF16)
    do_("dt", [INNER, TOWN], BF16)
    do_("z", [INNER, TOWN], BF16)
    do_("bm", [16, TOWN])
    do_("cm", [16, TOWN])
    do_("summ", [INNER, 32])

    mrow_d = nc.dram_tensor("mrow_d", [NE, EXT], F32, kind="Internal").ap()
    w1q_d = nc.dram_tensor("w1q_d", [NE, NT_F, 128, NT_D * 128], BF16, kind="Internal").ap()
    w2q_d = nc.dram_tensor("w2q_d", [NE, NT_D, 128, NT_F * 128], BF16, kind="Internal").ap()
    bm_d = nc.dram_tensor("bm_d", [16, TOWN], F32, kind="Internal").ap()

    with contextlib.ExitStack() as st0:
        al = nc.alloc_sbuf_tensor
        ones_c = al("ones_c", [128, 1], F32)
        ones_r = al("ones_r", [1, 128], F32)
        onesb_c = al("onesb_c", [128, 1], BF16)
        m3c = al("m3c", [128, 1], F32)
        iden = al("iden", [128, 128], F32)
        rows = al("rows", [1, 3 * CH], F32)
        cvc = al("cvc", [128, 2], F32)

        blk = st0.enter_context(nc.Block())
        P = Prog(nc, st0)

        @blk.sync
        def _(sy):
            P.dma(sy, "dy", m3c[:], I["m3col"])
            P.dma(sy, "dy", iden[:], I["idenf"])
            P.dma(sy, "dy", cvc[:], I["cvals"])
        d0 = P.cnt["dy"]

        @blk.vector
        def _(v):
            P.wait(v, "dy", d0)
            P.V(v.memset(ones_c[:], 1.0))
            P.V(v.memset(ones_r[:], 1.0))
            P.V(v.memset(onesb_c[:], 1.0))
        v_ones = P.cnt["v"]

        with contextlib.ExitStack() as _sc:
            xn = _sc.enter_context(nc.sbuf_tensor("xn", [128, NT_D * EXT], BF16))
            SW = 2052
            s_sb = _sc.enter_context(nc.sbuf_tensor("s_sb", [128, NT_D * SW], BF16))

            with contextlib.ExitStack() as _sc:
                xa = _sc.enter_context(nc.sbuf_tensor("xa", [128, NT_D * CH], F32))
                sqa = _sc.enter_context(nc.sbuf_tensor("sqa", [128, CH], F32))
                ps_s = _sc.enter_context(nc.psum_tensor("ps_s", [1, CH], F32))
                ps_q = _sc.enter_context(nc.psum_tensor("ps_q", [1, CH], F32))
                ps_m = _sc.enter_context(nc.psum_tensor("ps_m", [128, CH], F32))
                ps_r = _sc.enter_context(nc.psum_tensor("ps_r", [128, CH], F32))
                for ci in range(NCH):
                    c0 = ci * CH
                    pv = P.cnt["v"]

                    @blk.sync
                    def _(sy):
                        P.wait(sy, "v", pv)
                        for i in range(NT_D):
                            P.dma(sy, "dy", xa[:, i * CH:(i + 1) * CH],
                                  I["x_ext"][i * 128:(i + 1) * 128, c0:c0 + CH])
                    dA = P.cnt["dy"]

                    wm = {"dy": dA}
                    if ci == 0:
                        wm["v"] = v_ones
                    t_bc = emit_ln(nc, blk, P,
                                   lambda i: xa[:, i * CH:(i + 1) * CH], CH,
                                   rows, sqa, ps_s, ps_q, ps_m, ps_r,
                                   ones_c[:, 0:1], ones_r[0:1, :],
                                   wait_map=wm, eps_col=cvc[:, 0:1])

                    @blk.vector
                    def _(v):
                        P.wait(v, "t", t_bc)
                        for i in range(NT_D):
                            xt = xa[:, i * CH:(i + 1) * CH]
                            P.V(v.tensor_tensor(xt, xt, ps_m[:, 0:CH], ALU.subtract))
                            P.V(v.tensor_tensor(xn[:, i * EXT + c0:i * EXT + c0 + CH],
                                                xt, ps_r[:, 0:CH], ALU.mult))

            if DEBUG1:
                pv_va = P.cnt["v"]

                @blk.sync
                def _(sy):
                    P.wait(sy, "v", pv_va)
                    for i in range(NT_D):
                        P.dma(sy, "dy", O["dbg_xn"][i * 128:(i + 1) * 128, :],
                              xn[:, i * EXT:(i + 1) * EXT])

            fence(nc, blk, P)
            precast_jobs = []
            for e in range(NE):
                for ft in range(NT_F):
                    precast_jobs.append(("w1", e, ft))
                for dot in range(NT_D):
                    precast_jobs.append(("w2", e, dot))
            pre_cursor = [0]
            pre_done = [0] * NE

            def emit_precast(g, n):
                while n > 0 and pre_cursor[0] < len(precast_jobs):
                    kind, e, idx = precast_jobs[pre_cursor[0]]
                    if kind == "w1":
                        P.dma(g, "dg", w1q_d[e, idx], I["w1T"][e, idx])
                    else:
                        P.dma(g, "dg", w2q_d[e, idx], I["w2T"][e, idx])
                    pre_cursor[0] += 1
                    if pre_cursor[0] % (NT_F + NT_D) == 0:
                        pre_done[(pre_cursor[0] // (NT_F + NT_D)) - 1] = P.cnt["dg"]
                    n -= 1

            with contextlib.ExitStack() as _sc:
                wfb = _sc.enter_context(nc.sbuf_tensor("wfb", [128, 2 * 24 * 128], BF16))
                wob = _sc.enter_context(nc.sbuf_tensor("wob", [128, NT_D * NT_H * 128], BF16))
                eb1 = _sc.enter_context(nc.sbuf_tensor("eb1", [128, CH], F32))
                eb2 = _sc.enter_context(nc.sbuf_tensor("eb2", [128, CH], F32))
                rb = _sc.enter_context(nc.sbuf_tensor("rb", [128, CH], F32))
                gb = _sc.enter_context(nc.sbuf_tensor("gb", [128, CH], F32))
                fb = _sc.enter_context(nc.sbuf_tensor("fb", [128, CH], F32))
                vb = _sc.enter_context(nc.sbuf_tensor("vb", [128, CH], F32))
                hb = _sc.enter_context(nc.sbuf_tensor("hb", [128, CH], F32))
                hchunk = _sc.enter_context(nc.sbuf_tensor("hchunk", [128, NT_H * CH], BF16))
                carry = _sc.enter_context(nc.sbuf_tensor("carry", [128, NT_H], F32))
                ps_a = _sc.enter_context(nc.psum_tensor("ps_a", [128, CH], F32))
                ps_b = _sc.enter_context(nc.psum_tensor("ps_b", [128, CH], F32))
                ps_h = _sc.enter_context(nc.psum_tensor("ps_h", [128, CH], F32))
                ps_o = _sc.enter_context(nc.psum_tensor("ps_o", [128, CH], F32))
                ps_o2 = _sc.enter_context(nc.psum_tensor("ps_o2", [128, CH], F32))

                @blk.vector
                def _(v):
                    P.V(v.memset(carry[:], 0.0))

                @blk.gpsimd
                def _(g):
                    for dot in range(NT_D):
                        P.dma(g, "dg",
                              wob[:, dot * NT_H * 128:(dot + 1) * NT_H * 128],
                              I["woT"][dot])
                dg_wo = P.cnt["dg"]

                for ci in range(NCH):
                    c0 = ci * CH
                    for lt in range(NT_H):
                        par = (ci * NT_H + lt) % 2
                        wsl = wfb[:, par * 24 * 128:(par + 1) * 24 * 128]
                        pv_t = P.cnt["t"]

                        @blk.gpsimd
                        def _(g):
                            P.wait(g, "t", pv_t)
                            for sl in range(3):
                                P.dma(g, "dg",
                                      wsl[:, sl * 8 * 128:(sl + 1) * 8 * 128],
                                      I["wfT"][sl * NT_H + lt])
                            emit_precast(g, 4)
                        dg_w = P.cnt["dg"]
                        s_prev = P.cnt["s"]

                        @blk.tensor
                        def _(t):
                            P.wait(t, "dg", dg_w)
                            P.wait(t, "s", s_prev)
                            for sl, ps in ((0, ps_a), (1, ps_b), (2, ps_h)):
                                for kt in range(NT_D):
                                    inst = t.matmul(
                                        ps[:, 0:CH],
                                        wsl[:, (sl * 8 + kt) * 128:(sl * 8 + kt + 1) * 128],
                                        xn[:, kt * EXT + c0:kt * EXT + c0 + CH],
                                        start=(kt == 0), stop=(kt == NT_D - 1))
                                P.T(inst)
                        t_g = P.cnt["t"]
                        v_pb = P.cnt["v"]

                        s_e = [0]

                        @blk.scalar
                        def _(s):
                            P.wait(s, "t", t_g)
                            P.wait(s, "v", v_pb)
                            P.S(s.activation(eb1[:], ps_a[:, 0:CH], AF.Exp, scale=-1.0))
                            P.S(s.activation(eb2[:], ps_b[:, 0:CH], AF.Exp, scale=-1.0))
                            P.S(s.activation(gb[:], ps_h[:, 0:CH], AF.Exp, scale=-1.0))
                            P.S(s.activation(gb[:], gb[:], AF.Ln, bias=1.0))
                            s_e[0] = P.S(s.activation(gb[:], gb[:], AF.Exp, scale=-1.0))

                        v_d = [0]

                        @blk.vector
                        def _(v):
                            P.wait(v, "s", s_e[0])
                            P.V(v.scalar_tensor_tensor(rb[:], eb1[:], 2.0, eb2[:],
                                                       ALU.add, ALU.add))
                            v_d[0] = P.cnt["v"]

                        s_r = [0]

                        @blk.scalar
                        def _(s):
                            P.wait(s, "v", v_d[0])
                            P.S(s.activation(rb[:], rb[:], AF.Ln))
                            P.S(s.activation(rb[:], rb[:], AF.Exp, scale=-1.0))
                            s_r[0] = P.S(s.activation(fb[:], ps_h[:, 0:CH], AF.Relu))

                        @blk.vector
                        def _(v):
                            P.wait(v, "s", s_r[0])
                            P.V(v.tensor_scalar(gb[:], gb[:], 0.5, None, ALU.min))
                            P.V(v.tensor_tensor(gb[:], gb[:], fb[:], ALU.add))
                            P.V(v.scalar_tensor_tensor(fb[:], eb2[:], 1.0, rb[:],
                                                       ALU.add, ALU.mult))
                            P.V(v.scalar_tensor_tensor(vb[:], eb1[:], 1.0, rb[:],
                                                       ALU.add, ALU.mult))
                            P.V(v.tensor_tensor(vb[:], vb[:], gb[:], ALU.mult))
                            if ci == 0:
                                P.V(v.tensor_scalar(fb[:], fb[:], m3c[:, 0:1],
                                                    None, ALU.mult))
                                P.V(v.tensor_scalar(vb[:], vb[:], m3c[:, 0:1],
                                                    None, ALU.mult))
                            n_sc = P.V(v.tensor_tensor_scan(hb[:], fb[:], vb[:],
                                                     carry[:, lt:lt + 1],
                                                     ALU.mult, ALU.add))
                            P.wait(v, "v", n_sc)
                            P.V(v.tensor_copy(carry[:, lt:lt + 1], hb[:, CH - 1:CH]))
                            P.V(v.tensor_copy(hchunk[:, lt * CH:(lt + 1) * CH], hb[:]))

                    v_h = P.cnt["v"]
                    for dot in range(NT_D):
                        ps = ps_o if dot % 2 == 0 else ps_o2
                        pv_v = P.cnt["v"]
                        t_mm = [0]

                        @blk.tensor
                        def _(t):
                            P.wait(t, "v", v_h)
                            P.wait(t, "v", pv_v)
                            if ci == 0 and dot == 0:
                                P.wait(t, "dg", dg_wo)
                            for kt in range(NT_H):
                                inst = t.matmul(
                                    ps[:, 0:CH],
                                    wob[:, (dot * NT_H + kt) * 128:(dot * NT_H + kt + 1) * 128],
                                    hchunk[:, kt * CH:(kt + 1) * CH],
                                    start=(kt == 0), stop=(kt == NT_H - 1))
                            t_mm[0] = P.T(inst)

                        @blk.vector
                        def _(v):
                            P.wait(v, "t", t_mm[0])
                            if ci == 0:
                                P.V(v.tensor_copy(s_sb[:, dot * SW:dot * SW + 4],
                                                  ps[:, 508:512]))
                            else:
                                P.V(v.tensor_copy(
                                    s_sb[:, dot * SW + c0 - 508:
                                         dot * SW + c0 - 508 + CH],
                                    ps[:, 0:CH]))

            fence(nc, blk, P)
            with contextlib.ExitStack() as _sc:
                xhat = _sc.enter_context(nc.sbuf_tensor("xhat", [128, NT_D * EXT], BF16))
                centb = _sc.enter_context(nc.sbuf_tensor("centb", [128, NT_D * NE], BF16))
                gcols = _sc.enter_context(nc.sbuf_tensor("gcols", [128, NE * NT_D], F32))
                bcols = _sc.enter_context(nc.sbuf_tensor("bcols", [128, NE * NT_D], F32))
                b1cols = _sc.enter_context(nc.sbuf_tensor("b1cols", [128, NE * NT_F], F32))
                b2sb = _sc.enter_context(nc.sbuf_tensor("b2sb", [NE, D], F32))
                sqc = _sc.enter_context(nc.sbuf_tensor("sqc", [128, CH], F32))
                alph = _sc.enter_context(nc.sbuf_tensor("alph", [128, 2], F32))
                msk = _sc.enter_context(nc.sbuf_tensor("msk", [128, NE], F32))
                msk_ev = _sc.enter_context(nc.sbuf_tensor("msk_ev", [NE, 128], F32))
                w1b = _sc.enter_context(nc.sbuf_tensor("w1b", [128, 2 * 8 * 128], BF16))
                w2b = _sc.enter_context(nc.sbuf_tensor("w2b", [128, 2 * 16 * 128], BF16))
                heb = _sc.enter_context(nc.sbuf_tensor("heb", [128, 2 * NT_D * CH], BF16))
                ub = _sc.enter_context(nc.sbuf_tensor("ub", [128, NT_F * CH], BF16))
                accb = _sc.enter_context(nc.sbuf_tensor("accb", [128, NT_D * CH], BF16))
                msb = _sc.enter_context(nc.sbuf_tensor("msb", [128, 2 * CH], BF16))
                mr8 = _sc.enter_context(nc.sbuf_tensor("mr8", [NE, CH], F32))
                tfin = _sc.enter_context(nc.sbuf_tensor("tfin", [128, CH], F32))
                xrow = _sc.enter_context(nc.sbuf_tensor("xrow", [128, CH], F32))
                ps_u = _sc.enter_context(nc.psum_tensor("ps_u", [128, CH], F32))
                ps_u2 = _sc.enter_context(nc.psum_tensor("ps_u2", [128, CH], F32))
                ps_y = _sc.enter_context(nc.psum_tensor("ps_y", [128, CH], F32))
                ps_y2 = _sc.enter_context(nc.psum_tensor("ps_y2", [128, CH], F32))
                ps_k1 = _sc.enter_context(nc.psum_tensor("ps_k1", [128, CH], F32))
                ps_b2 = _sc.enter_context(nc.psum_tensor("ps_b2", [128, CH], F32))

                @blk.gpsimd
                def _(g):
                    for i in range(NT_D):
                        P.dma(g, "dg", centb[:, i * NE:(i + 1) * NE],
                              I["centT"][i * 128:(i + 1) * 128, :])
                    emit_precast(g, 10 ** 9)
                dg_c = P.cnt["dg"]
                assert pre_cursor[0] == len(precast_jobs) and all(pre_done)

                @blk.sync
                def _(sy):
                    P.dma(sy, "dy", gcols[:], I["gcol"])
                    P.dma(sy, "dy", bcols[:], I["bcol"])
                    P.dma(sy, "dy", b1cols[:], I["b1col"])
                    P.dma(sy, "dy", b2sb[:], I["b2w"])
                dy_c = P.cnt["dy"]

                _scp = contextlib.ExitStack()
                ps_sc = _scp.enter_context(nc.psum_tensor("ps_sc", [128, NE], F32))
                ps_tr = _scp.enter_context(nc.psum_tensor("ps_tr", [NE, 128], F32))
                for tsl in range(EXT // 128):
                    o = tsl * 128
                    pv_v = P.cnt["v"]
                    t_sc = [0]

                    @blk.tensor
                    def _(t):
                        if tsl == 0:
                            P.wait(t, "dg", dg_c)
                        P.wait(t, "v", pv_v)
                        for kt in range(NT_D):
                            inst = t.matmul(ps_sc[:, 0:NE],
                                            xn[:, kt * EXT + o:kt * EXT + o + 128],
                                            centb[:, kt * NE:(kt + 1) * NE],
                                            start=(kt == 0), stop=(kt == NT_D - 1))
                        t_sc[0] = P.T(inst)

                    v_mx = [0]

                    @blk.vector
                    def _(v):
                        P.wait(v, "t", t_sc[0])
                        P.V(v.tensor_reduce(alph[:, 0:1], ps_sc[:, 0:NE], AX.X,
                                            ALU.max))
                        v_mx[0] = P.cnt["v"]

                    s_al = [0]

                    @blk.scalar
                    def _(s):
                        P.wait(s, "v", v_mx[0])
                        P.S(s.activation(alph[:, 1:2], alph[:, 0:1], AF.Exp,
                                         scale=-1.0))
                        P.S(s.activation(alph[:, 1:2], alph[:, 1:2], AF.Ln,
                                         bias=1.0))
                        s_al[0] = P.S(s.activation(alph[:, 1:2], alph[:, 1:2],
                                                   AF.Exp, scale=-1.0))

                    v_m = [0]

                    @blk.vector
                    def _(v):
                        P.wait(v, "s", s_al[0])
                        P.V(v.tensor_scalar(msk[:], ps_sc[:, 0:NE], alph[:, 0:1],
                                            alph[:, 1:2], ALU.is_equal, ALU.mult))
                        v_m[0] = P.cnt["v"]

                    t_tr = [0]
                    pv_dy = P.cnt["dy"]

                    @blk.tensor
                    def _(t):
                        P.wait(t, "v", v_m[0])
                        t_tr[0] = P.T(t.transpose(ps_tr[:, :], msk[:], iden[:]))

                    s_tr = [0]

                    @blk.scalar
                    def _(s):
                        P.wait(s, "t", t_tr[0])
                        P.wait(s, "dy", pv_dy)
                        s_tr[0] = P.S(s.activation(msk_ev[:, :], ps_tr[:, :],
                                                   AF.Copy))

                    @blk.sync
                    def _(sy):
                        P.wait(sy, "s", s_tr[0])
                        P.dma(sy, "dy", mrow_d[:, o:o + 128], msk_ev[:, :])
                dy_mr = P.cnt["dy"]
                _scp.close()
                fence(nc, blk, P)

                with contextlib.ExitStack() as _sc:
                    ps_s2 = _sc.enter_context(nc.psum_tensor("ps_s2", [1, CH], F32))
                    ps_q2 = _sc.enter_context(nc.psum_tensor("ps_q2", [1, CH], F32))
                    for ci in range(NCH):
                        c0 = ci * CH
                        pv_v = P.cnt["v"]
                        t_bc = emit_ln(
                            nc, blk, P,
                            lambda i: xn[:, i * EXT + c0:i * EXT + c0 + CH], CH,
                            rows, sqc, ps_s2, ps_q2, ps_u, ps_y,
                            onesb_c[:, 0:1], ones_r[0:1, :],
                            wait_map={"v": pv_v}, ones_sq=ones_c[:, 0:1],
                            eps_col=cvc[:, 0:1])

                        @blk.vector
                        def _(v):
                            P.wait(v, "t", t_bc)
                            for i in range(NT_D):
                                P.V(v.tensor_tensor(
                                    sqc[:], xn[:, i * EXT + c0:i * EXT + c0 + CH],
                                    ps_u[:, 0:CH], ALU.subtract))
                                P.V(v.tensor_tensor(
                                    xhat[:, i * EXT + c0:i * EXT + c0 + CH],
                                    sqc[:], ps_y[:, 0:CH], ALU.mult))

                if DEBUG1:
                    pv_vb = P.cnt["v"]

                    @blk.sync
                    def _(sy):
                        P.wait(sy, "v", pv_vb)
                        for i in range(NT_D):
                            P.dma(sy, "dy",
                                  O["dbg_s"][i * 128:(i + 1) * 128, 508:EXT],
                                  s_sb[:, i * SW:(i + 1) * SW])
                            P.dma(sy, "dy", O["dbg_xhat"][i * 128:(i + 1) * 128, :],
                                  xhat[:, i * EXT:(i + 1) * EXT])
                        P.dma(sy, "dy", O["dbg_mrow"], mrow_d)

                for ci in range(NCH):
                    c0 = ci * CH
                    v_acc0 = [0]

                    @blk.vector
                    def _(v):
                        P.V(v.memset(accb[:], 0.0))
                        v_acc0[0] = P.cnt["v"]

                    pv_t0 = P.cnt["t"]
                    dy_mr8 = [0]

                    @blk.sync
                    def _(sy):
                        P.wait(sy, "t", pv_t0)
                        P.wait(sy, "dy", dy_mr)
                        P.dma(sy, "dy", mr8[:, 0:CH], mrow_d[:, c0:c0 + CH])
                        dy_mr8[0] = P.cnt["dy"]

                    v_he_arr = {}

                    def emit_prep(e):
                        par = e % 2
                        hslot = heb[:, par * NT_D * CH:(par + 1) * NT_D * CH]
                        mslot = msb[:, par * CH:(par + 1) * CH]
                        pv_t = P.cnt["t"]
                        dg_m = [0]

                        @blk.gpsimd
                        def _(g):
                            P.wait(g, "dy", dy_mr)
                            P.wait(g, "t", pv_t)
                            P.dma(g, "dg", rows[0:1, 0:CH],
                                  mrow_d[e:e + 1, c0:c0 + CH])
                            dg_m[0] = P.cnt["dg"]

                        pv_v2 = P.cnt["v"]
                        t_mb = [0]

                        @blk.tensor
                        def _(t):
                            P.wait(t, "dg", dg_m[0])
                            P.wait(t, "v", pv_v2)
                            if ci == 0 and e == 0:
                                P.wait(t, "dy", dy_c)
                            t_mb[0] = P.T(t.matmul(ps_k1[:, 0:CH], ones_r[0:1, :],
                                                   rows[0:1, 0:CH],
                                                   start=True, stop=True))

                        @blk.vector
                        def _(v):
                            P.wait(v, "t", t_mb[0])
                            P.V(v.tensor_copy(mslot, ps_k1[:, 0:CH]))
                            for i in range(NT_D):
                                P.V(v.tensor_scalar(
                                    hslot[:, i * CH:(i + 1) * CH],
                                    xhat[:, i * EXT + c0:i * EXT + c0 + CH],
                                    gcols[:, e * NT_D + i:e * NT_D + i + 1],
                                    bcols[:, e * NT_D + i:e * NT_D + i + 1],
                                    ALU.mult, ALU.add))
                            v_he_arr[e] = P.cnt["v"]

                    emit_prep(0)
                    for e in range(NE):
                        par_e = e % 2
                        hcur = heb[:, par_e * NT_D * CH:(par_e + 1) * NT_D * CH]
                        mcur = msb[:, par_e * CH:(par_e + 1) * CH]
                        v_he = [v_he_arr[e]]

                        for ft in range(NT_F):
                            par = ft % 2
                            wsl = w1b[:, par * 8 * 128:(par + 1) * 8 * 128]
                            psu = ps_u if ft % 2 == 0 else ps_u2
                            pv_t = P.cnt["t"]

                            @blk.sync
                            def _(sy):
                                P.wait(sy, "t", pv_t)
                                if ci == 0 and ft == 0:
                                    P.wait(sy, "dg", pre_done[e])
                                P.dma(sy, "dy", wsl, w1q_d[e, ft])
                            dg_1 = P.cnt["dy"]
                            s_uev = P.cnt["s"]
                            t_u = [0]

                            @blk.tensor
                            def _(t):
                                P.wait(t, "dy", dg_1)
                                P.wait(t, "v", v_he[0])
                                P.wait(t, "s", s_uev)
                                for kt in range(NT_D):
                                    inst = t.matmul(psu[:, 0:CH],
                                                    wsl[:, kt * 128:(kt + 1) * 128],
                                                    hcur[:, kt * CH:(kt + 1) * CH],
                                                    start=(kt == 0),
                                                    stop=(kt == NT_D - 1))
                                t_u[0] = P.T(inst)

                            @blk.scalar
                            def _(s):
                                P.wait(s, "t", t_u[0])
                                P.S(s.activation(
                                    ub[:, ft * CH:(ft + 1) * CH], psu[:, 0:CH],
                                    AF.Relu,
                                    bias=b1cols[:, e * NT_F + ft:e * NT_F + ft + 1]))
                        s_u = P.cnt["s"]
                        if e + 1 < NE:
                            emit_prep(e + 1)

                        for dot in range(NT_D):
                            ps = ps_y if dot % 2 == 0 else ps_y2
                            v_yev = P.cnt["v"]
                            t_y = [0]
                            for half in range(2):
                                par = half
                                w2sl = w2b[:, par * 16 * 128:(par + 1) * 16 * 128]
                                pv_t = P.cnt["t"]

                                @blk.sync
                                def _(sy):
                                    P.wait(sy, "t", pv_t)
                                    P.dma(sy, "dy", w2sl,
                                          w2q_d[e, dot][:,
                                              half * 16 * 128:(half + 1) * 16 * 128])
                                dg_2 = P.cnt["dy"]

                                @blk.tensor
                                def _(t):
                                    P.wait(t, "dy", dg_2)
                                    if half == 0:
                                        P.wait(t, "s", s_u)
                                        P.wait(t, "v", v_yev)
                                    for kt in range(16):
                                        gk = half * 16 + kt
                                        inst = t.matmul(
                                            ps[:, 0:CH],
                                            w2sl[:, kt * 128:(kt + 1) * 128],
                                            ub[:, gk * CH:(gk + 1) * CH],
                                            start=(gk == 0), stop=(gk == NT_F - 1))
                                    if half == 1:
                                        t_y[0] = P.T(inst)

                            @blk.vector
                            def _(v):
                                P.wait(v, "t", t_y[0])
                                P.V(v.tensor_tensor(sqc[:], ps[:, 0:CH], mcur,
                                                    ALU.mult))
                                P.V(v.tensor_tensor(
                                    accb[:, dot * CH:(dot + 1) * CH],
                                    accb[:, dot * CH:(dot + 1) * CH],
                                    sqc[:], ALU.add))

                    if DEBUG1:
                        pv_vc = P.cnt["v"]

                        @blk.sync
                        def _(sy):
                            P.wait(sy, "v", pv_vc)
                            for i in range(NT_D):
                                P.dma(sy, "dy",
                                      O["dbg_acc"][i * 128:(i + 1) * 128,
                                                   c0:c0 + CH],
                                      accb[:, i * CH:(i + 1) * CH])

                    dy_m8 = dy_mr8

                    fo = 508 if ci == 0 else 0
                    fw = 4 if ci == 0 else CH
                    so = c0 + fo - 508
                    for dot in range(NT_D):
                        pv_v = P.cnt["v"]
                        t_b2m = [0]

                        @blk.tensor
                        def _(t):
                            P.wait(t, "dy", dy_m8[0])
                            P.wait(t, "v", pv_v)
                            t_b2m[0] = P.T(t.matmul(
                                ps_b2[:, 0:fw], b2sb[:, dot * 128:(dot + 1) * 128],
                                mr8[:, fo:fo + fw], start=True, stop=True))

                        dy_x = [0]

                        @blk.sync
                        def _(sy):
                            P.dma(sy, "dy", xrow[:, 0:fw],
                                  I["x_ext"][dot * 128:(dot + 1) * 128,
                                             c0 + fo:c0 + fo + fw])
                            dy_x[0] = P.cnt["dy"]

                        v_t = [0]

                        @blk.vector
                        def _(v):
                            P.wait(v, "t", t_b2m[0])
                            P.V(v.tensor_tensor(tfin[:, 0:fw],
                                                accb[:, dot * CH + fo:dot * CH + fo + fw],
                                                ps_b2[:, 0:fw], ALU.add))
                            P.V(v.tensor_tensor(
                                tfin[:, 0:fw], tfin[:, 0:fw],
                                xn[:, dot * EXT + c0 + fo:dot * EXT + c0 + fo + fw],
                                ALU.add))
                            v_t[0] = P.cnt["v"]

                        s_g = [0]

                        @blk.scalar
                        def _(s):
                            P.wait(s, "v", v_t[0])
                            P.S(s.activation(sqc[:, 0:fw], tfin[:, 0:fw], AF.Exp,
                                             scale=-1.0))
                            P.S(s.activation(sqc[:, 0:fw], sqc[:, 0:fw], AF.Ln,
                                             bias=1.0))
                            s_g[0] = P.S(s.activation(sqc[:, 0:fw], sqc[:, 0:fw],
                                                      AF.Exp, scale=-1.0))

                        v_x1 = [0]

                        @blk.vector
                        def _(v):
                            P.wait(v, "s", s_g[0])
                            P.wait(v, "dy", dy_x[0])
                            P.V(v.tensor_tensor(tfin[:, 0:fw], tfin[:, 0:fw],
                                                sqc[:, 0:fw], ALU.mult))
                            P.V(v.tensor_tensor(
                                tfin[:, 0:fw], s_sb[:, dot * SW + so:dot * SW + so + fw],
                                tfin[:, 0:fw], ALU.mult))
                            P.V(v.tensor_tensor(xrow[:, 0:fw], xrow[:, 0:fw],
                                                tfin[:, 0:fw], ALU.add))
                            v_x1[0] = P.cnt["v"]

                        @blk.sync
                        def _(sy):
                            P.wait(sy, "v", v_x1[0])
                            P.dma(sy, "dy",
                                  O["x1"][dot * 128:(dot + 1) * 128,
                                          c0 + fo:c0 + fo + fw],
                                  xrow[:, 0:fw])

        fence(nc, blk, P)
        with nc.sbuf_tensor("xcsb", [128, NT_C * TOWN], BF16) as xcsb:
            with contextlib.ExitStack() as _sc:
                xn1 = _sc.enter_context(nc.sbuf_tensor("xn1", [128, NT_D * EXT], BF16))
                xa2 = _sc.enter_context(nc.sbuf_tensor("xa2", [128, NT_D * CH], F32))
                sqa2 = _sc.enter_context(nc.sbuf_tensor("sqa2", [128, CH], F32))
                ipbuf = _sc.enter_context(nc.sbuf_tensor("ipbuf", [128, 2 * 8 * 128], BF16))
                ipbsb = _sc.enter_context(nc.sbuf_tensor("ipbsb", [128, 32], F32))
                xp = _sc.enter_context(nc.sbuf_tensor("xp", [128, NT_C * 2052], BF16))
                zev = _sc.enter_context(nc.sbuf_tensor("zev", [128, CH], BF16))
                cwsb = _sc.enter_context(nc.sbuf_tensor("cwsb", [128, 4 * NT_C], F32))
                cbsb = _sc.enter_context(nc.sbuf_tensor("cbsb", [128, NT_C], F32))
                cvb = _sc.enter_context(nc.sbuf_tensor("cvb", [128, CH], F32))
                cv2 = _sc.enter_context(nc.sbuf_tensor("cv2", [128, CH], F32))
                cbneg = _sc.enter_context(nc.sbuf_tensor("cbneg", [128, NT_C], F32))
                ps_s3 = _sc.enter_context(nc.psum_tensor("ps_s3", [1, CH], F32))
                ps_q3 = _sc.enter_context(nc.psum_tensor("ps_q3", [1, CH], F32))
                ps_m3 = _sc.enter_context(nc.psum_tensor("ps_m3", [128, CH], F32))
                ps_r3 = _sc.enter_context(nc.psum_tensor("ps_r3", [128, CH], F32))
                ps_ip = _sc.enter_context(nc.psum_tensor("ps_ip", [128, CH], F32))
                ps_ip2 = _sc.enter_context(nc.psum_tensor("ps_ip2", [128, CH], F32))

                @blk.sync
                def _(sy):
                    P.dma(sy, "dy", ipbsb[:], I["ipb"])
                    P.dma(sy, "dy", cwsb[:], I["cw"])
                    P.dma(sy, "dy", cbsb[:], I["cbc"])
                dy_ip = P.cnt["dy"]

                @blk.vector
                def _(v):
                    P.wait(v, "dy", dy_ip)
                    P.V(v.tensor_scalar(cbneg[:], cbsb[:], -1.0, None, ALU.mult))
                    for i in range(NT_D):
                        P.V(v.memset(xn1[:, i * EXT:i * EXT + 508], 0.0))

                for ci in range(NCH):
                    lo = 508 if ci == 0 else ci * CH
                    lw = 4 if ci == 0 else CH
                    pv = P.cnt["v"]

                    @blk.sync
                    def _(sy):
                        P.wait(sy, "v", pv)
                        for i in range(NT_D):
                            P.dma(sy, "dy", xa2[:, i * CH:i * CH + lw],
                                  O["x1"][i * 128:(i + 1) * 128, lo:lo + lw])
                    dA = P.cnt["dy"]

                    t_bc = emit_ln(nc, blk, P,
                                   lambda i: xa2[:, i * CH:i * CH + lw], lw,
                                   rows, sqa2, ps_s3, ps_q3, ps_m3, ps_r3,
                                   ones_c[:, 0:1], ones_r[0:1, :],
                                   wait_map={"dy": dA}, eps_col=cvc[:, 0:1])

                    v_ln = [0]

                    @blk.vector
                    def _(v):
                        P.wait(v, "t", t_bc)
                        for i in range(NT_D):
                            xt = xa2[:, i * CH:i * CH + lw]
                            P.V(v.tensor_tensor(xt, xt, ps_m3[:, 0:lw], ALU.subtract))
                            P.V(v.tensor_tensor(xt, xt, ps_r3[:, 0:lw], ALU.mult))
                            P.V(v.tensor_copy(
                                xn1[:, i * EXT + lo:i * EXT + lo + lw], xt))
                        v_ln[0] = P.cnt["v"]

                    if ci >= 1:
                        @blk.sync
                        def _(sy):
                            P.wait(sy, "v", v_ln[0])
                            for i in range(NT_D):
                                P.dma(sy, "dy",
                                      O["xn1f"][i * 128:(i + 1) * 128,
                                                lo - CH:lo],
                                      xa2[:, i * CH:(i + 1) * CH])

                ip_chunks = [(508, 4)] + [(512 + k * CH, CH) for k in range(4)]
                for (c0, w) in ip_chunks:
                    for rt in range(32):
                        par = rt % 2
                        wsl = ipbuf[:, par * 8 * 128:(par + 1) * 8 * 128]
                        pv_t = P.cnt["t"]

                        @blk.gpsimd
                        def _(g):
                            P.wait(g, "t", pv_t)
                            P.dma(g, "dg", wsl,
                                  I["ipT"][rt])
                        dg_i = P.cnt["dg"]
                        ps = ps_ip if rt % 2 == 0 else ps_ip2
                        v_pv = P.cnt["v"]
                        t_ip = [0]

                        @blk.tensor
                        def _(t):
                            P.wait(t, "dg", dg_i)
                            P.wait(t, "v", v_pv)
                            if c0 == 508 and rt == 0:
                                P.wait(t, "dy", dy_ip)
                            for kt in range(NT_D):
                                inst = t.matmul(
                                    ps[:, 0:w], wsl[:, kt * 128:(kt + 1) * 128],
                                    xn1[:, kt * EXT + c0:kt * EXT + c0 + w],
                                    start=(kt == 0), stop=(kt == NT_D - 1))
                            t_ip[0] = P.T(inst)

                        if rt < NT_C:
                            @blk.vector
                            def _(v):
                                P.wait(v, "t", t_ip[0])
                                P.V(v.tensor_scalar(
                                    xp[:, rt * 2052 + (c0 - 508):
                                       rt * 2052 + (c0 - 508) + w],
                                    ps[:, 0:w], ipbsb[:, rt:rt + 1], None,
                                    ALU.add))
                        elif c0 != 508:
                            zt = rt - NT_C
                            s_z = [0]

                            pv_dyz = P.cnt["dy"]

                            @blk.vector
                            def _(v):
                                P.wait(v, "t", t_ip[0])
                                P.wait(v, "dy", pv_dyz)
                                s_z[0] = P.V(v.tensor_scalar(
                                    zev[:, 0:w], ps[:, 0:w],
                                    ipbsb[:, rt:rt + 1], None, ALU.add))

                            @blk.sync
                            def _(sy):
                                P.wait(sy, "v", s_z[0])
                                P.dma(sy, "dy",
                                      O["z"][zt * 128:(zt + 1) * 128,
                                             c0 - 512:c0 - 512 + w], zev[:, 0:w])

                @blk.vector
                def _(v):
                    for lt in range(NT_C):
                        P.V(v.tensor_scalar(xp[:, lt * 2052 + 1:lt * 2052 + 4],
                                            xp[:, lt * 2052 + 1:lt * 2052 + 4],
                                            m3c[:, 0:1], None, ALU.mult))

                for ci in range(4):
                    for lt in range(NT_C):
                        xpo = lt * 2052 + 4 + ci * CH
                        s_prevcv = P.cnt["s"]
                        v_cv = [0]

                        @blk.vector
                        def _(v):
                            P.wait(v, "s", s_prevcv)
                            P.V(v.tensor_scalar(cvb[:], xp[:, xpo - 3:xpo - 3 + CH],
                                                cwsb[:, lt * 4:lt * 4 + 1], None,
                                                ALU.mult))
                            for j in range(1, 4):
                                P.V(v.scalar_tensor_tensor(
                                    cvb[:], xp[:, xpo - 3 + j:xpo - 3 + j + CH],
                                    cwsb[:, lt * 4 + j:lt * 4 + j + 1], cvb[:],
                                    ALU.mult, ALU.add))
                            v_cv[0] = P.cnt["v"]

                        s_cv = [0]

                        @blk.scalar
                        def _(s):
                            P.wait(s, "v", v_cv[0])
                            P.S(s.activation(cv2[:], cvb[:], AF.Exp, scale=-1.0,
                                             bias=cbneg[:, lt:lt + 1]))
                            P.S(s.activation(cv2[:], cv2[:], AF.Ln, bias=1.0))
                            s_cv[0] = P.S(s.activation(cv2[:], cv2[:], AF.Exp,
                                                       scale=-1.0))

                        v_cv2 = [0]

                        @blk.vector
                        def _(v):
                            P.wait(v, "s", s_cv[0])
                            P.V(v.tensor_scalar(cvb[:], cvb[:], cbsb[:, lt:lt + 1],
                                                None, ALU.add))
                            v_cv2[0] = P.V(v.tensor_tensor(
                                xcsb[:, lt * TOWN + ci * CH:lt * TOWN + (ci + 1) * CH],
                                cvb[:], cv2[:], ALU.mult))

                        @blk.sync
                        def _(sy):
                            P.wait(sy, "v", v_cv2[0])
                            P.dma(sy, "dy",
                                  O["xc"][lt * 128:(lt + 1) * 128,
                                          ci * CH:(ci + 1) * CH],
                                  xcsb[:, lt * TOWN + ci * CH:lt * TOWN + (ci + 1) * CH])

            fence(nc, blk, P)
            with contextlib.ExitStack() as _sc:
                dtw = _sc.enter_context(nc.sbuf_tensor("dtw", [128, 2 * 16 * 128], BF16))
                dtbsb = _sc.enter_context(nc.sbuf_tensor("dtbsb", [128, NT_C], F32))
                t1 = _sc.enter_context(nc.sbuf_tensor("t1", [128, CH], F32))
                t2 = _sc.enter_context(nc.sbuf_tensor("t2", [128, CH], F32))
                dtsb = _sc.enter_context(nc.sbuf_tensor("dtsb", [128, NT_C * TOWN], BF16))
                dtot = _sc.enter_context(nc.sbuf_tensor("dtot", [128, NT_C], F32))
                asb = _sc.enter_context(nc.sbuf_tensor("asb", [128, STATE * NT_C], F32))
                bwsb = _sc.enter_context(nc.sbuf_tensor("bwsb", [128, NT_C * 16], BF16))
                cwsb2 = _sc.enter_context(nc.sbuf_tensor("cwsb2", [128, NT_C * 16], BF16))
                bbsb = _sc.enter_context(nc.sbuf_tensor("bbsb", [16, 1], F32))
                cbsb2 = _sc.enter_context(nc.sbuf_tensor("cbsb2", [16, 1], F32))
                bcr = _sc.enter_context(nc.sbuf_tensor("bcr", [16, CH], F32))
                brow1 = _sc.enter_context(nc.sbuf_tensor("brow1", [1, CH], F32))
                wbuf = _sc.enter_context(nc.sbuf_tensor("wbuf", [128, NT_C * CH], BF16))
                dab16 = _sc.enter_context(nc.sbuf_tensor("dab16", [128, 2 * CH], F32))
                d1b = _sc.enter_context(nc.sbuf_tensor("d1b", [128, CH], F32))
                hsc = _sc.enter_context(nc.sbuf_tensor("hsc", [128, CH], F32))
                st1 = _sc.enter_context(nc.sbuf_tensor("st1", [128, NT_C * STATE], F32))
                summ_sb = _sc.enter_context(nc.sbuf_tensor("summ_sb", [128, NT_C * 32], F32))
                ps_dt = _sc.enter_context(nc.psum_tensor("ps_dt", [128, CH], F32))
                ps_dt2 = _sc.enter_context(nc.psum_tensor("ps_dt2", [128, CH], F32))
                ps_bc = _sc.enter_context(nc.psum_tensor("ps_bc", [16, CH], F32))
                ps_bb = _sc.enter_context(nc.psum_tensor("ps_bb", [128, CH], F32))

                @blk.sync
                def _(sy):
                    P.dma(sy, "dy", dtbsb[:], I["dtb"])
                    P.dma(sy, "dy", bbsb[:], I["bbr"])
                    P.dma(sy, "dy", cbsb2[:], I["cbr"])
                    P.dma(sy, "dy", asb[:], I["alog"])
                dy_d = P.cnt["dy"]

                @blk.gpsimd
                def _(g):
                    for kt in range(NT_C):
                        P.dma(g, "dg", bwsb[:, kt * 16:(kt + 1) * 16], I["bwT"][kt])
                        P.dma(g, "dg", cwsb2[:, kt * 16:(kt + 1) * 16], I["cwTT"][kt])
                dg_bc = P.cnt["dg"]

                s_a = [0]

                @blk.scalar
                def _(s):
                    P.wait(s, "dy", dy_d)
                    s_a[0] = P.S(s.activation(asb[:], asb[:], AF.Exp))

                @blk.vector
                def _(v):
                    P.wait(v, "s", s_a[0])
                    P.V(v.tensor_scalar(asb[:], asb[:], -1.0, None, ALU.mult))
                    P.V(v.memset(dtot[:], 0.0))
                    P.V(v.memset(st1[:], 0.0))

                for ci in range(4):
                    c0 = ci * CH
                    for rt in range(NT_C):
                        par = rt % 2
                        wsl = dtw[:, par * 16 * 128:(par + 1) * 16 * 128]
                        pv_t = P.cnt["t"]

                        @blk.gpsimd
                        def _(g):
                            P.wait(g, "t", pv_t)
                            for piece in range(2):
                                P.dma(g, "dg",
                                      wsl[:, piece * 8 * 128:(piece + 1) * 8 * 128],
                                      I["dtT"][rt][:, piece * 8 * 128:(piece + 1) * 8 * 128])
                        dg_dt = P.cnt["dg"]
                        ps = ps_dt if rt % 2 == 0 else ps_dt2
                        v_pv = P.cnt["v"]
                        s_xc = P.cnt["s"]
                        t_dt = [0]

                        @blk.tensor
                        def _(t):
                            P.wait(t, "dg", dg_dt)
                            P.wait(t, "v", v_pv)
                            if ci == 0 and rt == 0:
                                P.wait(t, "s", s_xc)
                            for kt in range(NT_C):
                                inst = t.matmul(
                                    ps[:, 0:CH], wsl[:, kt * 128:(kt + 1) * 128],
                                    xcsb[:, kt * TOWN + c0:kt * TOWN + c0 + CH],
                                    start=(kt == 0), stop=(kt == NT_C - 1))
                            t_dt[0] = P.T(inst)

                        v_prevdve = P.cnt["v"]
                        s_sp = [0]

                        @blk.scalar
                        def _(s):
                            P.wait(s, "t", t_dt[0])
                            P.wait(s, "v", v_prevdve)
                            P.S(s.activation(t1[:], ps[:, 0:CH], AF.Abs,
                                             bias=dtbsb[:, rt:rt + 1]))
                            P.S(s.activation(t2[:], t1[:], AF.Exp, scale=-1.0))
                            s_sp[0] = P.S(s.activation(t2[:], t2[:], AF.Ln, bias=1.0))

                        @blk.vector
                        def _(v):
                            P.wait(v, "s", s_sp[0])
                            P.V(v.tensor_scalar(t1[:], ps[:, 0:CH],
                                                dtbsb[:, rt:rt + 1], 0.0,
                                                ALU.add, ALU.max))
                            dtt = dtsb[:, rt * TOWN + c0:rt * TOWN + c0 + CH]
                            P.V(v.tensor_tensor(dtt, t1[:], t2[:], ALU.add))
                            n_rd = P.V(v.tensor_reduce(t2[:, 0:1], dtt, AX.X,
                                                       ALU.add))
                            P.wait(v, "v", n_rd)
                            P.V(v.tensor_tensor(dtot[:, rt:rt + 1],
                                                dtot[:, rt:rt + 1], t2[:, 0:1],
                                                ALU.add))
                    v_dt = P.cnt["v"]

                    @blk.sync
                    def _(sy):
                        P.wait(sy, "v", v_dt)
                        for rt in range(NT_C):
                            P.dma(sy, "dy",
                                  O["dt"][rt * 128:(rt + 1) * 128, c0:c0 + CH],
                                  dtsb[:, rt * TOWN + c0:rt * TOWN + c0 + CH])

                    for which, wt, bias_c, out_d in (
                            (0, bwsb, bbsb, O["bm"]), (1, cwsb2, cbsb2, O["cm"])):
                        v_pv = P.cnt["v"]
                        t_b = [0]

                        @blk.tensor
                        def _(t):
                            if ci == 0 and which == 0:
                                P.wait(t, "dg", dg_bc)
                            P.wait(t, "v", v_pv)
                            for kt in range(NT_C):
                                inst = t.matmul(
                                    ps_bc[:, 0:CH], wt[:, kt * 16:(kt + 1) * 16],
                                    xcsb[:, kt * TOWN + c0:kt * TOWN + c0 + CH],
                                    start=(kt == 0), stop=(kt == NT_C - 1))
                            t_b[0] = P.T(inst)

                        v_bc = [0]
                        pv_dy2 = P.cnt["dy"]

                        @blk.vector
                        def _(v):
                            P.wait(v, "t", t_b[0])
                            P.wait(v, "dy", pv_dy2)
                            P.V(v.tensor_scalar(bcr[:, 0:CH], ps_bc[:, 0:CH],
                                                bias_c[:, 0:1], None, ALU.add))
                            v_bc[0] = P.cnt["v"]

                        @blk.sync
                        def _(sy):
                            P.wait(sy, "v", v_bc[0])
                            P.dma(sy, "dy", out_d[:, c0:c0 + CH], bcr[:, 0:CH])
                            if which == 0:
                                P.dma(sy, "dy", bm_d[:, c0:c0 + CH], bcr[:, 0:CH])

                for ci in range(4):
                    c0 = ci * CH
                    v_w = [0]
                    s_xc2 = P.cnt["s"]

                    @blk.vector
                    def _(v):
                        P.wait(v, "s", s_xc2)
                        for lt in range(NT_C):
                            P.V(v.tensor_tensor(
                                wbuf[:, lt * CH:(lt + 1) * CH],
                                dtsb[:, lt * TOWN + c0:lt * TOWN + c0 + CH],
                                xcsb[:, lt * TOWN + c0:lt * TOWN + c0 + CH],
                                ALU.mult))
                        v_w[0] = P.cnt["v"]

                    for si in range(STATE):
                        pv_t = P.cnt["t"]
                        dy_b = [0]

                        @blk.sync
                        def _(sy):
                            P.wait(sy, "t", pv_t)
                            P.dma(sy, "dy", brow1[0:1, 0:CH],
                                  bm_d[si:si + 1, c0:c0 + CH])
                            dy_b[0] = P.cnt["dy"]

                        pv_v = P.cnt["v"]
                        t_bb = [0]

                        @blk.tensor
                        def _(t):
                            P.wait(t, "dy", dy_b[0])
                            P.wait(t, "v", pv_v)
                            t_bb[0] = P.T(t.matmul(ps_bb[:, 0:CH], ones_r[0:1, :],
                                                   brow1[0:1, 0:CH],
                                                   start=True, stop=True))

                        for lt in range(NT_C):
                            dsl = dab16[:, (lt % 2) * CH:(lt % 2 + 1) * CH]
                            pv_v2 = P.cnt["v"]
                            s_da = [0]

                            @blk.scalar
                            def _(s):
                                P.wait(s, "v", pv_v2)
                                s_da[0] = P.S(s.activation(
                                    dsl,
                                    dtsb[:, lt * TOWN + c0:lt * TOWN + c0 + CH],
                                    AF.Exp,
                                    scale=asb[:, si * NT_C + lt:si * NT_C + lt + 1]))

                            @blk.vector
                            def _(v):
                                P.wait(v, "s", s_da[0])
                                P.wait(v, "t", t_bb[0])
                                P.wait(v, "v", v_w[0])
                                P.V(v.tensor_tensor(d1b[:],
                                                    wbuf[:, lt * CH:(lt + 1) * CH],
                                                    ps_bb[:, 0:CH], ALU.mult))
                                n_sc = P.V(v.tensor_tensor_scan(
                                    hsc[:], dsl, d1b[:],
                                    st1[:, lt * STATE + si:lt * STATE + si + 1],
                                    ALU.mult, ALU.add))
                                P.wait(v, "v", n_sc)
                                P.V(v.tensor_copy(
                                    st1[:, lt * STATE + si:lt * STATE + si + 1],
                                    hsc[:, CH - 1:CH]))

                s_pe = [0]
                pv_v = P.cnt["v"]

                @blk.scalar
                def _(s):
                    P.wait(s, "v", pv_v)
                    for si in range(STATE):
                        for lt in range(NT_C):
                            P.S(s.activation(
                                summ_sb[:, lt * 32 + si:lt * 32 + si + 1],
                                dtot[:, lt:lt + 1], AF.Exp,
                                scale=asb[:, si * NT_C + lt:si * NT_C + lt + 1]))
                    s_pe[0] = P.cnt["s"]

                v_fin = [0]

                @blk.vector
                def _(v):
                    P.wait(v, "s", s_pe[0])
                    for lt in range(NT_C):
                        P.V(v.tensor_copy(
                            summ_sb[:, lt * 32 + 16:lt * 32 + 32],
                            st1[:, lt * STATE:(lt + 1) * STATE]))
                    v_fin[0] = P.cnt["v"]

                @blk.sync
                def _(sy):
                    P.wait(sy, "v", v_fin[0])
                    for lt in range(NT_C):
                        P.dma(sy, "dy", O["summ"][lt * 128:(lt + 1) * 128, :],
                              summ_sb[:, lt * 32:(lt + 1) * 32])

        @blk.sync
        def _(sy):
            P.wait(sy, "dy", P.cnt["dy"])

        @blk.gpsimd
        def _(g):
            P.wait(g, "dg", P.cnt["dg"])
            P.wait(g, "dy", P.cnt["dy"])

    return nc



def build_launch2():
    nc = bass.Bass("TRN2", target_bir_lowering=False)
    I, O = {}, {}

    def di(name, shape, dt=F32):
        I[name] = nc.dram_tensor(name, list(shape), dt, kind="ExternalInput").ap()

    def do_(name, shape, dt=F32):
        O[name] = nc.dram_tensor(name, list(shape), dt, kind="ExternalOutput").ap()

    di("x1o", [D, TOWN])
    di("xn1f", [D, TOWN])
    di("xci", [INNER, TOWN], BF16)
    di("dti", [INNER, TOWN], BF16)
    di("zi", [INNER, TOWN], BF16)
    di("bmi", [16, TOWN])
    di("cmi", [16, TOWN])
    di("summs", [4, INNER, 32])
    di("selc", [128, 4])
    di("alog", [128, STATE * NT_C])
    di("dcol", [128, NT_C])
    di("owT", [NT_D, 128, INNER])
    di("owb", [128, NT_D])
    di("wg", [FFN, D])
    di("wu", [FFN, D])
    di("wd", [D, FFN])
    di("idenf", [128, 128])
    di("cvals", [128, 2])

    do_("x2", [D, TOWN])
    if DEBUG2:
        do_("dbg_s1", [D, TOWN], BF16)
        do_("dbg_y", [INNER, TOWN], BF16)
        do_("dbg_xqf", [D, TOWN], BF16)
        do_("dbg_hqf", [FFN, TOWN], BF16)
        do_("dbg_wgq", [NT_FF, 128, NT_D * 128], BF16)

    wgq_d = nc.dram_tensor("wgq_d", [NT_FF, 128, NT_D * 128], BF16, kind="Internal").ap()
    wuq_d = nc.dram_tensor("wuq_d", [NT_FF, 128, NT_D * 128], BF16, kind="Internal").ap()
    wdq_d = nc.dram_tensor("wdq_d", [NT_D, 128, NT_FF * 128], BF16, kind="Internal").ap()

    with contextlib.ExitStack() as st0:
        al = nc.alloc_sbuf_tensor
        ones_r = al("ones_r", [1, 128], F32)
        iden = al("iden", [128, 128], F32)
        idenb = al("idenb", [128, 128], BF16)
        rows = al("rows", [1, 3 * CH], F32)
        s1_sb = al("s1_sb", [128, NT_D * TOWN], BF16)
        ones_c = al("ones_c", [128, 1], F32)
        cvc = al("cvc", [128, 2], F32)

        blk = st0.enter_context(nc.Block())
        P = Prog(nc, st0)

        @blk.sync
        def _(sy):
            P.dma(sy, "dy", iden[:], I["idenf"])
            P.dma(sy, "dy", cvc[:], I["cvals"])
        d0 = P.cnt["dy"]

        @blk.gpsimd
        def _(g):
            P.dma(g, "dg", idenb[:], I["idenf"])
        d0g = P.cnt["dg"]

        @blk.vector
        def _(v):
            P.wait(v, "dy", d0)
            P.V(v.memset(ones_r[:], 1.0))
            P.V(v.memset(ones_c[:], 1.0))

        with contextlib.ExitStack() as _sc:
            summ_sb = _sc.enter_context(nc.sbuf_tensor("summ_sb", [128, NT_C * 32], F32))
            hin = _sc.enter_context(nc.sbuf_tensor("hin", [128, NT_C * STATE], F32))
            hnew = _sc.enter_context(nc.sbuf_tensor("hnew", [128, STATE], F32))
            selsb = _sc.enter_context(nc.sbuf_tensor("selsb", [128, 4], F32))
            asb = _sc.enter_context(nc.sbuf_tensor("asb", [128, STATE * NT_C], F32))
            dcsb = _sc.enter_context(nc.sbuf_tensor("dcsb", [128, NT_C], F32))
            owbsb = _sc.enter_context(nc.sbuf_tensor("owbsb", [128, NT_D], F32))
            owsb = _sc.enter_context(nc.sbuf_tensor("owsb", [128, NT_D * NT_C * 128], BF16))
            xcb = _sc.enter_context(nc.sbuf_tensor("xcb", [128, NT_C * CH], BF16))
            dtb_ = _sc.enter_context(nc.sbuf_tensor("dtb_", [128, NT_C * CH], BF16))
            wbuf = _sc.enter_context(nc.sbuf_tensor("wbuf", [128, NT_C * CH], BF16))
            ybuf = _sc.enter_context(nc.sbuf_tensor("ybuf", [128, NT_C * CH], F32))
            zbuf = _sc.enter_context(nc.sbuf_tensor("zbuf", [128, CH], BF16))
            szb = _sc.enter_context(nc.sbuf_tensor("szb", [128, CH], F32))
            brow1 = _sc.enter_context(nc.sbuf_tensor("brow1", [1, CH], F32))
            crow1 = _sc.enter_context(nc.sbuf_tensor("crow1", [1, CH], F32))
            dab16 = _sc.enter_context(nc.sbuf_tensor("dab16", [128, 2 * CH], F32))
            d1b = _sc.enter_context(nc.sbuf_tensor("d1b", [128, CH], F32))
            hsc = _sc.enter_context(nc.sbuf_tensor("hsc", [128, CH], F32))
            ytmp = _sc.enter_context(nc.sbuf_tensor("ytmp", [128, CH], F32))
            ybf = _sc.enter_context(nc.sbuf_tensor("ybf", [128, NT_C * CH], BF16))
            ps_bb = _sc.enter_context(nc.psum_tensor("ps_bb", [128, CH], F32))
            ps_cb = _sc.enter_context(nc.psum_tensor("ps_cb", [128, CH], F32))
            ps_s1 = _sc.enter_context(nc.psum_tensor("ps_s1", [128, CH], F32))
            ps_s1b = _sc.enter_context(nc.psum_tensor("ps_s1b", [128, CH], F32))

            @blk.sync
            def _(sy):
                P.dma(sy, "dy", selsb[:], I["selc"])
                P.dma(sy, "dy", asb[:], I["alog"])
                P.dma(sy, "dy", dcsb[:], I["dcol"])
                P.dma(sy, "dy", owbsb[:], I["owb"])
            dy_a = P.cnt["dy"]

            @blk.gpsimd
            def _(g):
                for dot in range(NT_D):
                    P.dma(g, "dg",
                          owsb[:, dot * NT_C * 128:(dot + 1) * NT_C * 128],
                          I["owT"][dot])
            dg_ow = P.cnt["dg"]

            s_a = [0]

            @blk.scalar
            def _(s):
                P.wait(s, "dy", dy_a)
                s_a[0] = P.S(s.activation(asb[:], asb[:], AF.Exp))

            @blk.vector
            def _(v):
                P.wait(v, "s", s_a[0])
                P.V(v.tensor_scalar(asb[:], asb[:], -1.0, None, ALU.mult))
                P.V(v.memset(hin[:], 0.0))

            for r in range(3):
                pv_v = P.cnt["v"]
                dy_s = [0]

                @blk.sync
                def _(sy):
                    P.wait(sy, "v", pv_v)
                    for lt in range(NT_C):
                        P.dma(sy, "dy", summ_sb[:, lt * 32:(lt + 1) * 32],
                              I["summs"][r, lt * 128:(lt + 1) * 128, :])
                    dy_s[0] = P.cnt["dy"]

                @blk.vector
                def _(v):
                    P.wait(v, "dy", dy_s[0])
                    for lt in range(NT_C):
                        hv = hin[:, lt * STATE:(lt + 1) * STATE]
                        P.V(v.tensor_tensor(hnew[:, 0:STATE],
                                            summ_sb[:, lt * 32:lt * 32 + 16],
                                            hv, ALU.mult))
                        P.V(v.tensor_tensor(hnew[:, 0:STATE], hnew[:, 0:STATE],
                                            summ_sb[:, lt * 32 + 16:lt * 32 + 32],
                                            ALU.add))
                        P.V(v.tensor_tensor(hnew[:, 0:STATE], hnew[:, 0:STATE],
                                            hv, ALU.subtract))
                        P.V(v.tensor_scalar(hnew[:, 0:STATE], hnew[:, 0:STATE],
                                            selsb[:, r:r + 1], None, ALU.mult))
                        P.V(v.tensor_tensor(hv, hv, hnew[:, 0:STATE], ALU.add))

            for ci in range(4):
                c0 = ci * CH
                pv_v = P.cnt["v"]
                dy_x = [0]

                @blk.sync
                def _(sy):
                    P.wait(sy, "v", pv_v)
                    for lt in range(NT_C):
                        P.dma(sy, "dy", xcb[:, lt * CH:(lt + 1) * CH],
                              I["xci"][lt * 128:(lt + 1) * 128, c0:c0 + CH])
                        P.dma(sy, "dy", dtb_[:, lt * CH:(lt + 1) * CH],
                              I["dti"][lt * 128:(lt + 1) * 128, c0:c0 + CH])
                    dy_x[0] = P.cnt["dy"]

                v_w = [0]

                @blk.vector
                def _(v):
                    P.wait(v, "dy", dy_x[0])
                    for lt in range(NT_C):
                        P.V(v.tensor_tensor(wbuf[:, lt * CH:(lt + 1) * CH],
                                            dtb_[:, lt * CH:(lt + 1) * CH],
                                            xcb[:, lt * CH:(lt + 1) * CH], ALU.mult))
                    P.V(v.memset(ybuf[:], 0.0))
                    v_w[0] = P.cnt["v"]

                for si in range(STATE):
                    pv_t = P.cnt["t"]
                    dy_b = [0]

                    @blk.sync
                    def _(sy):
                        P.wait(sy, "t", pv_t)
                        P.dma(sy, "dy", brow1[0:1, 0:CH],
                              I["bmi"][si:si + 1, c0:c0 + CH])
                        P.dma(sy, "dy", crow1[0:1, 0:CH],
                              I["cmi"][si:si + 1, c0:c0 + CH])
                        dy_b[0] = P.cnt["dy"]

                    pv_v2 = P.cnt["v"]
                    t_bb = [0]

                    @blk.tensor
                    def _(t):
                        P.wait(t, "dy", dy_b[0])
                        P.wait(t, "v", pv_v2)
                        P.T(t.matmul(ps_bb[:, 0:CH], ones_r[0:1, :],
                                     brow1[0:1, 0:CH], start=True, stop=True))
                        t_bb[0] = P.T(t.matmul(ps_cb[:, 0:CH], ones_r[0:1, :],
                                               crow1[0:1, 0:CH], start=True,
                                               stop=True))

                    for lt in range(NT_C):
                        dsl = dab16[:, (lt % 2) * CH:(lt % 2 + 1) * CH]
                        pv_v3 = P.cnt["v"]
                        s_da = [0]

                        @blk.scalar
                        def _(s):
                            P.wait(s, "v", pv_v3)
                            s_da[0] = P.S(s.activation(
                                dsl, dtb_[:, lt * CH:(lt + 1) * CH], AF.Exp,
                                scale=asb[:, si * NT_C + lt:si * NT_C + lt + 1]))

                        @blk.vector
                        def _(v):
                            P.wait(v, "s", s_da[0])
                            P.wait(v, "t", t_bb[0])
                            P.wait(v, "v", v_w[0])
                            P.V(v.tensor_tensor(d1b[:],
                                                wbuf[:, lt * CH:(lt + 1) * CH],
                                                ps_bb[:, 0:CH], ALU.mult))
                            n_sc = P.V(v.tensor_tensor_scan(
                                hsc[:], dsl, d1b[:],
                                hin[:, lt * STATE + si:lt * STATE + si + 1],
                                ALU.mult, ALU.add))
                            P.wait(v, "v", n_sc)
                            P.V(v.tensor_copy(
                                hin[:, lt * STATE + si:lt * STATE + si + 1],
                                hsc[:, CH - 1:CH]))
                            P.V(v.tensor_tensor(ytmp[:], hsc[:], ps_cb[:, 0:CH],
                                                ALU.mult))
                            yb = ybuf[:, lt * CH:(lt + 1) * CH]
                            P.V(v.tensor_tensor(yb, yb, ytmp[:], ALU.add))

                for lt in range(NT_C):
                    dy_z = [0]

                    @blk.sync
                    def _(sy):
                        pv_s = P.cnt["s"]
                        P.wait(sy, "s", pv_s)
                        P.dma(sy, "dy", zbuf[:, 0:CH],
                              I["zi"][lt * 128:(lt + 1) * 128, c0:c0 + CH])
                        dy_z[0] = P.cnt["dy"]

                    s_z = [0]

                    @blk.scalar
                    def _(s):
                        P.wait(s, "dy", dy_z[0])
                        P.S(s.activation(szb[:], zbuf[:, 0:CH], AF.Exp, scale=-1.0))
                        P.S(s.activation(szb[:], szb[:], AF.Ln, bias=1.0))
                        s_z[0] = P.S(s.activation(szb[:], szb[:], AF.Exp, scale=-1.0))

                    @blk.vector
                    def _(v):
                        P.wait(v, "s", s_z[0])
                        P.V(v.tensor_tensor(szb[:], szb[:], zbuf[:, 0:CH], ALU.mult))
                        yb = ybuf[:, lt * CH:(lt + 1) * CH]
                        P.V(v.scalar_tensor_tensor(yb, xcb[:, lt * CH:(lt + 1) * CH],
                                                   dcsb[:, lt:lt + 1], yb,
                                                   ALU.mult, ALU.add))
                        P.V(v.tensor_tensor(ybf[:, lt * CH:(lt + 1) * CH],
                                            yb, szb[:], ALU.mult))
                v_y = P.cnt["v"]
                if DEBUG2:
                    @blk.sync
                    def _(sy):
                        P.wait(sy, "v", v_y)
                        for lt in range(NT_C):
                            P.dma(sy, "dy",
                                  O["dbg_y"][lt * 128:(lt + 1) * 128, c0:c0 + CH],
                                  ybf[:, lt * CH:(lt + 1) * CH])

                for dot in range(NT_D):
                    ps = ps_s1 if dot % 2 == 0 else ps_s1b
                    pv_v9 = P.cnt["v"]
                    t_o = [0]

                    @blk.tensor
                    def _(t):
                        P.wait(t, "v", v_y)
                        P.wait(t, "v", pv_v9)
                        if ci == 0 and dot == 0:
                            P.wait(t, "dg", dg_ow)
                        for kt in range(NT_C):
                            inst = t.matmul(
                                ps[:, 0:CH],
                                owsb[:, (dot * NT_C + kt) * 128:(dot * NT_C + kt + 1) * 128],
                                ybf[:, kt * CH:(kt + 1) * CH],
                                start=(kt == 0), stop=(kt == NT_C - 1))
                        t_o[0] = P.T(inst)

                    @blk.vector
                    def _(v):
                        P.wait(v, "t", t_o[0])
                        P.V(v.tensor_scalar(
                            s1_sb[:, dot * TOWN + c0:dot * TOWN + c0 + CH],
                            ps[:, 0:CH], owbsb[:, dot:dot + 1], None, ALU.add))

        if DEBUG2:
            pv_vD = P.cnt["v"]

            @blk.sync
            def _(sy):
                P.wait(sy, "v", pv_vD)
                for i in range(NT_D):
                    P.dma(sy, "dy", O["dbg_s1"][i * 128:(i + 1) * 128, :],
                          s1_sb[:, i * TOWN:(i + 1) * TOWN])

        fence(nc, blk, P)
        with contextlib.ExitStack() as _sc:
            wtile = _sc.enter_context(nc.sbuf_tensor("wtile", [128, FFN], F32))
            qtile = _sc.enter_context(nc.sbuf_tensor("qtile", [128, FFN], BF16))
            sc = _sc.enter_context(nc.sbuf_tensor("sc", [128, 3 * (FFN // 16)], F32))
            tb = _sc.enter_context(nc.sbuf_tensor("tb", [128, FFN], F32))
            ab = _sc.enter_context(nc.sbuf_tensor("ab", [128, FFN], F32))
            ub2 = _sc.enter_context(nc.sbuf_tensor("ub2", [128, FFN], F32))
            qb = _sc.enter_context(nc.sbuf_tensor("qb", [128, FFN], F32))
            ttile = _sc.enter_context(nc.sbuf_tensor("ttile", [128, 128], BF16))
            ps_t = _sc.enter_context(nc.psum_tensor("ps_t", [128, 128], BF16))

            def quant_rows(src_d, n_rows_t, width, dst_d):
                for rt in range(n_rows_t):
                    pv_v = P.cnt["v"]
                    pv_s = P.cnt["s"]
                    dy_w = [0]

                    @blk.sync
                    def _(sy):
                        P.wait(sy, "v", pv_v)
                        P.wait(sy, "s", pv_s)
                        P.dma(sy, "dy", wtile[:, 0:width],
                              src_d[rt * 128:(rt + 1) * 128, :])
                        dy_w[0] = P.cnt["dy"]

                    emit_fp4(nc, blk, P, wtile[:, 0:width], qtile[:, 0:width],
                             width, sc, tb, ab, ub2, qb, cvc[:, 1:2],
                             wait_map={"dy": dy_w[0], "t": P.cnt["t"]})
                    v_q = P.cnt["v"]

                    for ct in range(width // 128):
                        t_t = [0]
                        pv_s2 = P.cnt["s"]

                        @blk.tensor
                        def _(t):
                            P.wait(t, "dg", d0g)
                            P.wait(t, "v", v_q)
                            P.wait(t, "s", pv_s2)
                            t_t[0] = P.T(t.transpose(
                                ps_t[:, :], qtile[:, ct * 128:(ct + 1) * 128],
                                idenb[:]))

                        s_t = [0]

                        pv_dyt = P.cnt["dy"]

                        @blk.scalar
                        def _(s):
                            P.wait(s, "t", t_t[0])
                            P.wait(s, "dy", pv_dyt)
                            s_t[0] = P.S(s.activation(ttile[:], ps_t[:, :], AF.Copy))

                        @blk.sync
                        def _(sy):
                            P.wait(sy, "s", s_t[0])
                            P.dma(sy, "dy", dst_d[rt][:, ct * 128:(ct + 1) * 128],
                                  ttile[:])

            @blk.vector
            def _(v):
                P.wait(v, "dg", d0g)

            quant_rows(I["wg"], NT_FF, D, wgq_d)
            quant_rows(I["wu"], NT_FF, D, wuq_d)
            quant_rows(I["wd"], NT_D, FFN, wdq_d)
        dy_wq = P.cnt["dy"]

        if DEBUG2:
            @blk.sync
            def _(sy):
                P.wait(sy, "dy", P.cnt["dy"])
                for i in range(NT_FF):
                    P.dma(sy, "dy", O["dbg_wgq"][i], wgq_d[i])

        fence(nc, blk, P)
        CHD = 256
        NTT = CHD // 128
        with contextlib.ExitStack() as _sc:
            x1b = _sc.enter_context(nc.sbuf_tensor("x1b", [128, NT_D * CHD], F32))
            xnf = _sc.enter_context(nc.sbuf_tensor("xnf", [128, NT_D * CHD], F32))
            xn_tm = _sc.enter_context(nc.sbuf_tensor("xn_tm", [128, NTT * D], F32))
            xq_tm = _sc.enter_context(nc.sbuf_tensor("xq_tm", [128, NTT * D], BF16))
            xqf = _sc.enter_context(nc.sbuf_tensor("xqf", [128, NT_D * CHD], BF16))
            sc2 = _sc.enter_context(nc.sbuf_tensor("sc2", [128, 3 * (FFN // 16)], F32))
            tb2 = _sc.enter_context(nc.sbuf_tensor("tb2", [128, FFN], F32))
            ab2 = _sc.enter_context(nc.sbuf_tensor("ab2", [128, FFN], F32))
            ub3 = _sc.enter_context(nc.sbuf_tensor("ub3", [128, FFN], F32))
            qb2 = _sc.enter_context(nc.sbuf_tensor("qb2", [128, FFN], F32))
            wqb = _sc.enter_context(nc.sbuf_tensor("wqb", [128, 2 * 8 * 128], BF16))
            sgb = _sc.enter_context(nc.sbuf_tensor("sgb", [128, CHD], BF16))
            sgt = _sc.enter_context(nc.sbuf_tensor("sgt", [128, CHD], F32))
            upb2 = _sc.enter_context(nc.sbuf_tensor("upb2", [128, CHD], F32))
            upb = _sc.enter_context(nc.sbuf_tensor("upb", [128, CHD], BF16))
            hfp = _sc.enter_context(nc.sbuf_tensor("hfp", [128, CHD], F32))
            h_tm = _sc.enter_context(nc.sbuf_tensor("h_tm", [128, NTT * FFN], F32))
            hq_tm = _sc.enter_context(nc.sbuf_tensor("hq_tm", [128, NTT * FFN], BF16))
            hqf = _sc.enter_context(nc.sbuf_tensor("hqf", [128, NT_FF * CHD], BF16))
            g1b = _sc.enter_context(nc.sbuf_tensor("g1b", [128, CHD], BF16))
            x2b = _sc.enter_context(nc.sbuf_tensor("x2b", [128, CHD], F32))
            ps_g = _sc.enter_context(nc.psum_tensor("ps_g", [128, CHD], F32))
            ps_g2 = _sc.enter_context(nc.psum_tensor("ps_g2", [128, CHD], F32))
            ps_tt = _sc.enter_context(nc.psum_tensor("ps_tt", [128, 512], F32))
            ps_tb = _sc.enter_context(nc.psum_tensor("ps_tb", [128, 512], BF16))

            for ci in range(TOWN // CHD):
                c0 = ci * CHD
                pv_v = P.cnt["v"]
                pv_s0 = P.cnt["s"]
                pv_t0 = P.cnt["t"]
                dy_x = [0]

                @blk.sync
                def _(sy):
                    P.wait(sy, "v", pv_v)
                    P.wait(sy, "s", pv_s0)
                    for i in range(NT_D):
                        P.dma(sy, "dy", x1b[:, i * CHD:(i + 1) * CHD],
                              I["x1o"][i * 128:(i + 1) * 128, c0:c0 + CHD])
                        P.dma(sy, "dy", xnf[:, i * CHD:(i + 1) * CHD],
                              I["xn1f"][i * 128:(i + 1) * 128, c0:c0 + CHD])
                    dy_x[0] = P.cnt["dy"]

                for tt in range(NTT):
                    for i4 in range(NT_D // 4):
                        pv_s = P.cnt["s"]
                        t_tt = [0]

                        @blk.tensor
                        def _(t):
                            P.wait(t, "dy", dy_x[0])
                            P.wait(t, "s", pv_s)
                            for j in range(4):
                                i = i4 * 4 + j
                                t_tt[0] = P.T(t.transpose(
                                    ps_tt[:, j * 128:(j + 1) * 128],
                                    xnf[:, i * CHD + tt * 128:i * CHD + (tt + 1) * 128],
                                    iden[:]))

                        @blk.scalar
                        def _(s):
                            P.wait(s, "t", t_tt[0])
                            if tt == 0 and i4 == 0:
                                P.wait(s, "v", pv_v)
                            P.S(s.activation(
                                xn_tm[:, tt * D + i4 * 512:tt * D + (i4 + 1) * 512],
                                ps_tt[:, 0:512], AF.Copy))
                s_tm = P.cnt["s"]

                for tt in range(NTT):
                    emit_fp4(nc, blk, P, xn_tm[:, tt * D:(tt + 1) * D],
                             xq_tm[:, tt * D:(tt + 1) * D], D, sc2, tb2, ab2,
                             ub3, qb2, wait_map={"s": s_tm, "t": pv_t0})
                v_q = P.cnt["v"]

                for tt in range(NTT):
                    for i4 in range(NT_D // 4):
                        pv_s = P.cnt["s"]
                        t_tb = [0]

                        @blk.tensor
                        def _(t):
                            P.wait(t, "v", v_q)
                            P.wait(t, "s", pv_s)
                            for j in range(4):
                                i = i4 * 4 + j
                                t_tb[0] = P.T(t.transpose(
                                    ps_tb[:, j * 128:(j + 1) * 128],
                                    xq_tm[:, tt * D + i * 128:tt * D + (i + 1) * 128],
                                    idenb[:]))

                        @blk.scalar
                        def _(s):
                            P.wait(s, "t", t_tb[0])
                            if tt == 0 and i4 == 0:
                                P.wait(s, "t", pv_t0)
                            for j in range(4):
                                i = i4 * 4 + j
                                P.S(s.activation(
                                    xqf[:, i * CHD + tt * 128:i * CHD + (tt + 1) * 128],
                                    ps_tb[:, j * 128:(j + 1) * 128], AF.Copy))
                s_xq = P.cnt["s"]
                if DEBUG2:
                    @blk.sync
                    def _(sy):
                        P.wait(sy, "s", s_xq)
                        for i in range(NT_D):
                            P.dma(sy, "dy",
                                  O["dbg_xqf"][i * 128:(i + 1) * 128, c0:c0 + CHD],
                                  xqf[:, i * CHD:(i + 1) * CHD])

                for ft in range(NT_FF):
                    par = ft % 2
                    wsl = wqb[:, par * 8 * 128:(par + 1) * 8 * 128]
                    for which, src_d in ((0, wgq_d), (1, wuq_d)):
                        pv_t = P.cnt["t"]

                        @blk.sync
                        def _(sy):
                            P.wait(sy, "t", pv_t)
                            if ci == 0 and ft == 0 and which == 0:
                                P.wait(sy, "dy", dy_wq)
                            P.dma(sy, "dy", wsl, src_d[ft])
                        dy_wg = P.cnt["dy"]
                        ps = ps_g if which == 0 else ps_g2
                        pv_s3 = P.cnt["s"]
                        t_gu = [0]

                        @blk.tensor
                        def _(t):
                            P.wait(t, "dy", dy_wg)
                            P.wait(t, "s", s_xq)
                            P.wait(t, "s", pv_s3)
                            for kt in range(NT_D):
                                inst = t.matmul(ps[:, 0:CHD],
                                                wsl[:, kt * 128:(kt + 1) * 128],
                                                xqf[:, kt * CHD:(kt + 1) * CHD],
                                                start=(kt == 0), stop=(kt == NT_D - 1))
                            t_gu[0] = P.T(inst)

                        @blk.scalar
                        def _(s):
                            P.wait(s, "t", t_gu[0])
                            if which == 0:
                                pv_vh = P.cnt["v"]
                                P.wait(s, "v", pv_vh)
                                P.S(s.activation(sgt[:], ps[:, 0:CHD], AF.Exp,
                                                 scale=-1.0))
                                P.S(s.activation(sgt[:], sgt[:], AF.Ln, bias=1.0))
                                P.S(s.activation(sgt[:], sgt[:], AF.Exp, scale=-1.0))
                                P.S(s.activation(upb2[:], ps[:, 0:CHD], AF.Copy))
                            else:
                                P.S(s.activation(upb[:], ps[:, 0:CHD], AF.Copy))
                    s_gu = P.cnt["s"]
                    v_h = [0]

                    pv_thf = P.cnt["t"]

                    @blk.vector
                    def _(v):
                        P.wait(v, "s", s_gu)
                        P.wait(v, "t", pv_thf)
                        P.V(v.tensor_tensor(hfp[:], sgt[:], upb2[:], ALU.mult))
                        P.V(v.tensor_tensor(hfp[:], hfp[:], upb[:], ALU.mult))
                        v_h[0] = P.cnt["v"]

                    pv_s4 = P.cnt["s"]
                    t_h = [0]

                    @blk.tensor
                    def _(t):
                        P.wait(t, "v", v_h[0])
                        P.wait(t, "s", pv_s4)
                        for tt in range(NTT):
                            t_h[0] = P.T(t.transpose(
                                ps_tt[:, tt * 128:(tt + 1) * 128],
                                hfp[:, tt * 128:(tt + 1) * 128],
                                iden[:]))

                    @blk.scalar
                    def _(s):
                        P.wait(s, "t", t_h[0])
                        if ft == 0:
                            P.wait(s, "v", pv_v)
                        for tt in range(NTT):
                            P.S(s.activation(
                                h_tm[:, tt * FFN + ft * 128:tt * FFN + (ft + 1) * 128],
                                ps_tt[:, tt * 128:(tt + 1) * 128], AF.Copy))
                s_htm = P.cnt["s"]
                pv_t1 = P.cnt["t"]

                for tt in range(NTT):
                    emit_fp4(nc, blk, P, h_tm[:, tt * FFN:(tt + 1) * FFN],
                             hq_tm[:, tt * FFN:(tt + 1) * FFN], FFN, sc2, tb2,
                             ab2, ub3, qb2, wait_map={"s": s_htm, "t": pv_t1})
                v_hq = P.cnt["v"]

                for f2 in range(NT_FF // 2):
                    pv_s5 = P.cnt["s"]
                    t_hb = [0]

                    @blk.tensor
                    def _(t):
                        P.wait(t, "v", v_hq)
                        P.wait(t, "s", pv_s5)
                        for j in range(2):
                            ft = f2 * 2 + j
                            for tt in range(NTT):
                                t_hb[0] = P.T(t.transpose(
                                    ps_tb[:, (j * NTT + tt) * 128:(j * NTT + tt + 1) * 128],
                                    hq_tm[:, tt * FFN + ft * 128:tt * FFN + (ft + 1) * 128],
                                    idenb[:]))

                    @blk.scalar
                    def _(s):
                        P.wait(s, "t", t_hb[0])
                        if f2 == 0:
                            P.wait(s, "t", pv_t1)
                        for j in range(2):
                            ft = f2 * 2 + j
                            for tt in range(NTT):
                                P.S(s.activation(
                                    hqf[:, ft * CHD + tt * 128:ft * CHD + (tt + 1) * 128],
                                    ps_tb[:, (j * NTT + tt) * 128:(j * NTT + tt + 1) * 128],
                                    AF.Copy))
                s_hq = P.cnt["s"]
                if DEBUG2:
                    @blk.sync
                    def _(sy):
                        P.wait(sy, "s", s_hq)
                        for i in range(NT_FF):
                            P.dma(sy, "dy",
                                  O["dbg_hqf"][i * 128:(i + 1) * 128, c0:c0 + CHD],
                                  hqf[:, i * CHD:(i + 1) * CHD])

                for dot in range(NT_D):
                    par = dot % 2
                    wsl = wqb[:, par * 8 * 128:(par + 1) * 8 * 128]
                    ps = ps_g if dot % 2 == 0 else ps_g2
                    pv_s6 = P.cnt["s"]
                    t_dn = [0]
                    for grp in range(3):
                        ksl = list(range(grp * 8, min((grp + 1) * 8, NT_FF)))
                        pv_t = P.cnt["t"]

                        @blk.sync
                        def _(sy):
                            P.wait(sy, "t", pv_t)
                            P.dma(sy, "dy", wsl[:, 0:len(ksl) * 128],
                                  wdq_d[dot][:, ksl[0] * 128:(ksl[-1] + 1) * 128])
                        dy_wd = P.cnt["dy"]

                        @blk.tensor
                        def _(t):
                            P.wait(t, "dy", dy_wd)
                            if grp == 0:
                                P.wait(t, "s", s_hq)
                                P.wait(t, "s", pv_s6)
                            for j, kt in enumerate(ksl):
                                inst = t.matmul(ps[:, 0:CHD],
                                                wsl[:, j * 128:(j + 1) * 128],
                                                hqf[:, kt * CHD:(kt + 1) * CHD],
                                                start=(kt == 0),
                                                stop=(kt == NT_FF - 1))
                            t_dn[0] = P.T(inst)

                    s_g1 = [0]

                    @blk.scalar
                    def _(s):
                        P.wait(s, "t", t_dn[0])
                        pv_vg = P.cnt["v"]
                        P.wait(s, "v", pv_vg)
                        P.S(s.activation(sgt[:], ps[:, 0:CHD], AF.Exp, scale=-1.0))
                        P.S(s.activation(sgt[:], sgt[:], AF.Ln, bias=1.0))
                        P.S(s.activation(sgt[:], sgt[:], AF.Exp, scale=-1.0))
                        s_g1[0] = P.S(s.activation(upb2[:], ps[:, 0:CHD], AF.Copy))

                    v_x2 = [0]

                    pv_dyx = P.cnt["dy"]

                    @blk.vector
                    def _(v):
                        P.wait(v, "s", s_g1[0])
                        P.wait(v, "dy", pv_dyx)
                        P.V(v.tensor_tensor(g1b[:], sgt[:], upb2[:], ALU.mult))
                        P.V(v.tensor_tensor(x2b[:],
                                            s1_sb[:, dot * TOWN + c0:dot * TOWN + c0 + CHD],
                                            g1b[:], ALU.mult))
                        P.V(v.tensor_tensor(x2b[:], x2b[:],
                                            x1b[:, dot * CHD:(dot + 1) * CHD],
                                            ALU.add))
                        v_x2[0] = P.cnt["v"]

                    @blk.sync
                    def _(sy):
                        P.wait(sy, "v", v_x2[0])
                        P.dma(sy, "dy", O["x2"][dot * 128:(dot + 1) * 128, c0:c0 + CHD],
                              x2b[:])

        @blk.sync
        def _(sy):
            P.wait(sy, "dy", P.cnt["dy"])

        @blk.gpsimd
        def _(g):
            P.wait(g, "dg", P.cnt["dg"])
            P.wait(g, "dy", P.cnt["dy"])

    return nc



_CACHE = {}


def kernel(**inputs):
    x = np.asarray(inputs["L0_attn_x"], np.float32)
    H = host_prepare(inputs)

    if "nc1" not in _CACHE:
        _CACHE["nc1"] = build_launch1()
        _CACHE["nc2"] = build_launch2()
    nc1, nc2 = _CACHE["nc1"], _CACHE["nc2"]

    H["cvals"] = np.ascontiguousarray(
        np.stack([np.full(128, EPS, np.float32),
                  np.full(128, MAGIC, np.float32)], 1))
    shared1 = {k: H[k] for k in
               ("idenf", "cvals", "wfT", "woT", "centT", "gcol", "bcol", "b1col", "b2w",
                "w1T", "w2T", "ipT", "ipb", "cw", "cbc", "dtT", "dtb", "alog",
                "bwT", "cwTT", "bbr", "cbr")}
    in_maps1 = []
    for c in range(8):
        b, q = divmod(c, 4)
        start = q * TOWN
        xe = np.zeros((EXT, D), np.float32)
        lo = start - WARM
        src_lo = max(lo, 0)
        xe[src_lo - lo:] = x[b, src_lo:start + TOWN]
        m = dict(shared1)
        m["x_ext"] = np.ascontiguousarray(xe.T)
        m["m3col"] = np.full((128, 1), 0.0 if q == 0 else 1.0, np.float32)
        in_maps1.append(m)

    res1 = run_bass_kernel_spmd(nc1, in_maps1, core_ids=list(range(8)))
    r1 = res1.results

    shared2 = {k: H[k] for k in ("alog", "dcol", "owT", "owb", "wg", "wu",
                                 "wd", "idenf", "cvals")}
    in_maps2 = []
    for c in range(8):
        b, q = divmod(c, 4)
        m = dict(shared2)
        m["x1o"] = np.ascontiguousarray(
            np.asarray(r1[c]["x1"])[:, WARM:])
        m["xn1f"] = np.asarray(r1[c]["xn1f"])
        m["xci"] = np.asarray(r1[c]["xc"])
        m["dti"] = np.asarray(r1[c]["dt"])
        m["zi"] = np.asarray(r1[c]["z"])
        m["bmi"] = np.asarray(r1[c]["bm"])
        m["cmi"] = np.asarray(r1[c]["cm"])
        m["summs"] = np.ascontiguousarray(
            np.stack([np.asarray(r1[4 * b + r]["summ"]) for r in range(4)]))
        sel = np.zeros((128, 4), np.float32)
        sel[:, :q] = 1.0
        m["selc"] = sel
        in_maps2.append(m)

    res2 = run_bass_kernel_spmd(nc2, in_maps2, core_ids=list(range(8)))
    out = np.empty((2, 8192, D), np.float32)
    for c in range(8):
        b, q = divmod(c, 4)
        out[b, q * TOWN:(q + 1) * TOWN] = np.asarray(res2.results[c]["x2"]).T
    return out
